# version 20
# baseline (speedup 1.0000x reference)
import numpy as np
from contextlib import ExitStack
import concourse.bass as bass
import concourse.mybir as mybir
from concourse.bass_utils import run_bass_kernel_spmd

F32 = mybir.dt.float32
BF16 = mybir.dt.bfloat16
I32 = mybir.dt.int32
AF = mybir.ActivationFunctionType
AXX = mybir.AxisListType.X
ALU = mybir.AluOpType

D = 1024
S = 2048
NB = 2
NH = 16
HD = 64
FH = 3072
NT = S // 128
NJ = D // 128
NHB = FH // 128
WHB = 12
WV_TANH = True
WVON = True
WV_DISCARD = False
WV_NOACT = False
WV_NOMM = False
WV_NODQ = False
EPS = 1e-5
INV127 = float(np.float32(1.0) / np.float32(127.0))


class Buf:
    __slots__ = ("w", "r")

    def __init__(self):
        self.w = None
        self.r = {}


class MK:
    ENG = ("pe", "dv", "ac", "sp", "gp")
    NLANES = {"sp": 8, "gp": 4, "ac": 2}

    def __init__(self):
        self.th = {e: [] for e in self.ENG}
        self.cnt = {e: 0 for e in self.ENG}
        for q, nl in self.NLANES.items():
            for l in range(nl):
                self.cnt[f"{q}{l}"] = 0
        self.dmacnt = {"sp": 0, "gp": 0, "ac": 0}
        self.pending = {}

    def all_sems(self):
        return list(self.cnt)

    def emit(self, eng, fn=None, reads=(), writes=(), acc_first=(), acc_last=(),
             extra=(), dma=False):
        waits = {}

        def need(tok):
            if tok is not None:
                s, v = tok
                if waits.get(s, 0) < v:
                    waits[s] = v

        for b in reads:
            need(b.w)
        for b in tuple(writes) + tuple(acc_first):
            need(b.w)
            for s, v in b.r.items():
                need((s, v))
        for t in extra:
            need(t)
        p = self.pending.pop(eng, None)
        if p:
            for s, v in p.items():
                need((s, v))
        if dma:
            i = self.dmacnt[eng]
            self.dmacnt[eng] = i + 1
            lane = i % self.NLANES[eng]
            semn = f"{eng}{lane}"
            if self.cnt[semn] > 0:
                need((semn, self.cnt[semn]))
            self.cnt[semn] += 16
            tok = (semn, self.cnt[semn])
            inc = (semn, 16)
        else:
            self.cnt[eng] += 1
            tok = (eng, self.cnt[eng])
            inc = (eng, 1)
        for b in reads:
            if b.r.get(tok[0], 0) < tok[1]:
                b.r[tok[0]] = tok[1]
        for b in tuple(writes) + tuple(acc_last):
            b.w = tok
            b.r = {}
        self.th[eng].append((waits, fn, inc))
        return tok

    def barrier(self):
        caps = {s: v for s, v in self.cnt.items() if v > 0}
        for e in self.ENG:
            p = self.pending.setdefault(e, {})
            for s, v in caps.items():
                if p.get(s, 0) < v:
                    p[s] = v

    def final_wait(self, eng):
        cap = dict(self.cnt)
        self.th[eng].append(
            ({s: v for s, v in cap.items() if v > 0}, None, None))


def build_program():
    nc = bass.Bass("TRN2", target_bir_lowering=False, monotonic_sem_count=0)
    d_x = nc.dram_tensor("x", [NB * S, D], F32, kind="ExternalInput")
    d_wq = nc.dram_tensor("wq", [D, D], F32, kind="ExternalInput")
    d_wk = nc.dram_tensor("wk", [D, D], F32, kind="ExternalInput")
    d_wv = nc.dram_tensor("wv", [D, D], F32, kind="ExternalInput")
    d_wo = nc.dram_tensor("wo", [D, D], F32, kind="ExternalInput")
    d_bo = nc.dram_tensor("bo", [1, D], F32, kind="ExternalInput")
    d_wg = nc.dram_tensor("wg", [FH, D], I32, kind="ExternalInput")
    d_wu = nc.dram_tensor("wu", [FH, D], I32, kind="ExternalInput")
    d_wd = nc.dram_tensor("wd", [FH, D], I32, kind="ExternalInput")
    d_id = nc.dram_tensor("ident", [128, 128], F32, kind="ExternalInput")
    d_y = nc.dram_tensor("y", [NB * S, D], F32, kind="ExternalOutput")

    es = ExitStack()
    SB = lambda nm, sh, dt: es.enter_context(nc.sbuf_tensor(nm, sh, dt))
    mk = MK()
    E = mk.emit

    xnT = SB("xnT", [128, NJ * S], BF16)
    QT = SB("QT", [128, NJ * S], BF16)
    xin = [SB(f"xin{i}", [128, D], F32) for i in range(2)]
    xr = [SB(f"xr{i}", [128, 512], F32) for i in range(2)]
    out4 = [SB(f"out{i}", [128, 512], F32) for i in range(3)]
    ident_b = SB("idb", [128, 128], BF16)
    ones_row = SB("ones", [1, 128], BF16)
    bo_b = SB("bob", [1, D], BF16)
    negc = SB("negc", [128, NH], F32)
    stat = [SB(f"stat{i}", [128, 16], F32) for i in range(2)]
    xnb = [SB(f"xnb{i}", [128, D], BF16) for i in range(2)]
    gs = None
    cm_f = [SB(f"cmf{i}", [128, 1], F32) for i in range(2)]
    cm_b = [SB(f"cmb{i}", [128, 1], BF16) for i in range(2)]
    c_f = [SB(f"cf{i}", [1, 1], F32) for i in range(2)]
    c_b = [SB(f"cb{i}", [1, 1], BF16) for i in range(2)]
    rc_b = [SB(f"rcb{i}", [1, 512], BF16) for i in range(2)]
    rc_f = [out4[0][0:1, :], out4[1][0:1, :]]
    b_rc_f = [None, None]
    pbig = [es.enter_context(nc.psum_tensor(f"pbig{i}", [128, 1024], F32))
            for i in range(2)]
    psm = [es.enter_context(nc.psum_tensor(f"psm{i}", [128, 512], F32))
           for i in range(4)]

    def BK(k, r0=0, r1=128, c0=0, c1=512):
        if k < 4:
            t = pbig[k // 2]
            off = (k % 2) * 512
            return t[r0:r1, off + c0: off + c1]
        return psm[k - 4][r0:r1, c0:c1]
    sems = {s: es.enter_context(nc.semaphore(f"sem_{s}")) for s in mk.all_sems()}

    B = Buf
    b_xin = [B(), B()]
    b_stat = [[B() for _ in range(8)] for _ in range(2)]
    b_xnb = [B(), B()]
    b_xnT = {(j, tc): B() for j in range(NJ) for tc in range(4)}
    b_QT = {(j, qc): B() for j in range(NJ) for qc in range(4)}
    b_pb = [B() for _ in range(8)]
    b_ident_f, b_ident_b, b_ones = B(), B(), B()
    b_bo_f, b_bo_b = B(), B()
    b_negc = [B() for _ in range(NH)]
    b_gs = [B(), B()]
    b_cm_f = [B(), B()]
    b_cm_b = [B(), B()]
    b_c_f = [B(), B()]
    b_c_b = [B(), B()]
    b_rc_b = [B(), B()]
    b_av_sb = [B(), B()]
    b_xr = [B(), B()]
    b_out4 = [B() for _ in range(3)]
    b_rc_f[0] = b_out4[0]
    b_rc_f[1] = b_out4[1]

    E("sp", dma=True, fn=lambda e: e.dma_start(out=ident_f, in_=d_id.ap()),
      writes=[b_ident_f, b_xin[1]])
    bo_f = xin[0][0:1, :]
    ident_f = xin[1][:, 0:128]
    E("sp", dma=True, fn=lambda e: e.dma_start(out=bo_f, in_=d_bo.ap()),
      writes=[b_bo_f, b_xin[0]])
    E("dv", lambda e: e.tensor_copy(ident_b[:], ident_f),
      reads=[b_ident_f, b_xin[1]], writes=[b_ident_b])
    E("dv", lambda e: e.memset(ones_row[:], 1.0), writes=[b_ones])
    E("dv", lambda e: e.tensor_copy(bo_b[:], bo_f),
      reads=[b_bo_f, b_xin[0]], writes=[b_bo_b])

    ROW = NJ * S

    def make_ln(n):
            def ln_s1(tb):
                i = tb % 2
                xi = tb % 2
                st_ = b_stat[i]
                E("sp", dma=True, fn=lambda e, xi=xi, tb=tb: e.dma_start(
                    out=xin[xi][:],
                    in_=d_x.ap()[n * S + tb * 128: n * S + tb * 128 + 128, :]),
                  writes=[b_xin[xi]])
                E("dv", lambda e, i=i, xi=xi: e.bn_stats(
                    stat[i][:, 0:6], xin[xi][:, 0:512]),
                  reads=[b_xin[xi]], writes=[st_[0]])
                E("dv", lambda e, i=i, xi=xi: e.bn_stats(
                    stat[i][:, 6:12], xin[xi][:, 512:1024]),
                  reads=[b_xin[xi]], writes=[st_[1]])
                E("dv", lambda e, i=i: e.bn_aggr(
                    stat[i][:, 12:14], stat[i][:, 0:12]),
                  reads=[st_[0], st_[1]], writes=[st_[2]])
                E("dv", lambda e, i=i: e.tensor_scalar(
                    stat[i][:, 13:14], stat[i][:, 13:14], EPS, None, ALU.add),
                  reads=[st_[2]], writes=[st_[3]])

            def ln_s2(tb):
                i = tb % 2
                xi = tb % 2
                st_ = b_stat[i]
                E("ac", lambda e, i=i: e.activation(
                    out=stat[i][:, 14:15], in_=stat[i][:, 13:14], func=AF.Sqrt),
                  reads=[st_[3]], writes=[st_[4]])
                E("dv", lambda e, i=i: e.reciprocal(
                    stat[i][:, 15:16], stat[i][:, 14:15]),
                  reads=[st_[4]], writes=[st_[5]])
                E("dv", lambda e, i=i: e.scalar_tensor_tensor(
                    stat[i][:, 14:15], stat[i][:, 12:13], -1.0, stat[i][:, 15:16],
                    ALU.mult, ALU.mult),
                  reads=[st_[2], st_[5]], writes=[st_[6]])
                E("ac", lambda e, i=i, xi=xi: e.activation(
                    out=xnb[i][:], in_=xin[xi][:], func=AF.Identity,
                    bias=stat[i][:, 14:15], scale=stat[i][:, 15:16]),
                  reads=[b_xin[xi], st_[5], st_[6]], writes=[b_xnb[i]])

            def ln_s3(tb):
                i = tb % 2
                bA, bB = 6, 7
                for half, bk in ((0, bA), (1, bB)):
                    for q in range(4):
                        db = half * 4 + q
                        E("pe", lambda e, i=i, db=db, bk=bk, q=q: e.matmul(
                            BK(bk, c0=q * 128, c1=(q + 1) * 128),
                            xnb[i][:, db * 128:(db + 1) * 128], ident_b[:],
                            start=True, stop=True),
                          reads=[b_xnb[i], b_ident_b],
                          acc_first=[b_pb[bk]] if q == 0 else (),
                          acc_last=[b_pb[bk]] if q == 3 else ())
                    eng = "dv" if half == 0 else "ac"
                    if eng == "dv":
                        E("dv", lambda e, tb=tb, half=half, bk=bk: e.tensor_copy(
                            bass.AP(xnT, half * 4 * S + tb * 128,
                                    [[ROW, 128], [S, 4], [1, 128]]),
                            BK(bk)),
                          reads=[b_pb[bk]],
                          writes=[b_xnT[(half * 4 + d, tb // 4)] for d in range(4)])
                    else:
                        E("ac", lambda e, tb=tb, half=half, bk=bk: e.copy(
                            bass.AP(xnT, half * 4 * S + tb * 128,
                                    [[ROW, 128], [S, 4], [1, 128]]),
                            BK(bk)),
                          reads=[b_pb[bk]],
                          writes=[b_xnT[(half * 4 + d, tb // 4)] for d in range(4)])

            def ln_slot(tb, vhook=None):
                if tb < NT:
                    ln_s1(tb)
                if tb >= 1 and tb - 1 < NT:
                    ln_s2(tb - 1)
                if tb >= 2 and tb - 2 < NT:
                    ln_s3(tb - 2)
                    if vhook is not None:
                        vhook(tb - 2)

            def ln_pipeline(vhook=None):
                for tb in range(NT + 2):
                    ln_slot(tb, vhook)

            return ln_slot, ln_pipeline

    def emit_elem(n, weave_next=None, own_ln=True):
        ln_slot, ln_pipeline = make_ln(n)
        b_H = {}
        for hb in range(NHB):
            for tc in range(2):
                _b = B()
                _b.r = {"pe": mk.cnt["pe"], "dv": mk.cnt["dv"]}
                b_H[(hb, tc)] = _b
        att = ExitStack()
        KT = att.enter_context(nc.sbuf_tensor(f"KT{n}", [128, NJ * S], BF16))
        Vaug = att.enter_context(
            nc.sbuf_tensor(f"Vaug{n}", [128, NT * NH * 65 + 64], BF16))
        expS = att.enter_context(
            nc.sbuf_tensor(f"expS{n}", [128, 4 * 1024], BF16))
        av_sb = [att.enter_context(
            nc.sbuf_tensor(f"avsb{n}_{i}", [65, 512], F32)) for i in range(2)]
        att_w = ExitStack()
        wbuf = att_w.enter_context(
            nc.sbuf_tensor(f"wbuf{n}", [128, 8 * 1024], BF16))
        wstage = [att_w.enter_context(
            nc.sbuf_tensor(f"wst{n}_{i}", [128, 1024], F32)) for i in range(3)]
        b_KT = {(j, tc): B() for j in range(NJ) for tc in range(4)}
        b_Vaug = [B() for _ in range(NT)]
        b_wslot = [B() for _ in range(8)]
        b_wstage = [B(), B(), B()]
        b_expS = [B() for _ in range(4)]
        VROW = NT * NH * 65
        VPITCH = VROW + 64

        def att_scope_open():
            E("dv", lambda e: e.memset(
                bass.AP(Vaug, 64, [[VPITCH, 128], [65, NT * NH]]), 1.0),
              writes=b_Vaug)
            E("dv", lambda e: e.memset(Vaug[:, VROW:VROW + 64], 0.0),
              writes=[b_Vaug[NT - 1]])

            def vload(jb):
                E("sp", dma=True, fn=lambda e, jb=jb: e.dma_start(
                    out=wstage[jb % 3][:],
                    in_=d_wv.ap()[jb * 128:(jb + 1) * 128, :]),
                  writes=[b_wstage[jb % 3]])
            vload(0)
            vload(1)
            for jb in range(NJ):
                E("dv", lambda e, jb=jb: e.tensor_copy(
                    wbuf[:, jb * 1024:(jb + 1) * 1024], wstage[jb % 3][:]),
                  reads=[b_wstage[jb % 3]], writes=[b_wslot[jb]])
                if jb + 2 < NJ:
                    vload(jb + 2)

        def v_pass_tb(tb):
            for dc in range(2):
                bk = (tb * 2 + dc) % 2
                for jb in range(NJ):
                    E("pe", lambda e, tb=tb, dc=dc, jb=jb, bk=bk: e.matmul(
                        BK(bk),
                        xnT[:, jb * S + tb * 128: jb * S + tb * 128 + 128],
                        wbuf[:, jb * 1024 + dc * 512:
                             jb * 1024 + dc * 512 + 512],
                        start=(jb == 0), stop=(jb == NJ - 1)),
                      reads=[b_wslot[jb], b_xnT[(jb, tb // 4)]],
                      acc_first=[b_pb[bk]] if jb == 0 else (),
                      acc_last=[b_pb[bk]] if jb == NJ - 1 else ())
                E("dv", lambda e, tb=tb, dc=dc, bk=bk: e.tensor_copy(
                    bass.AP(Vaug, tb * NH * 65 + dc * 8 * 65,
                            [[VPITCH, 128], [65, 8], [1, 64]]),
                    BK(bk)),
                  reads=[b_pb[bk]], writes=[b_Vaug[tb]])

        if n == 0:
            mk.barrier()
            att_scope_open()
            ln_pipeline(vhook=v_pass_tb)
        else:
            if own_ln:
                ln_pipeline(None)
            mk.barrier()
            att_scope_open()
            for tb in range(NT):
                v_pass_tb(tb)

        def proj(dram, dest, destbufs, kscale, hook=None):
            def wload(ib):
                E("sp", dma=True, fn=lambda e, ib=ib: e.dma_start(
                    out=wstage[ib % 3][:],
                    in_=dram.ap()[ib * 128:(ib + 1) * 128, :]),
                  writes=[b_wstage[ib % 3]])
            wload(0)
            wload(1)
            for ib in range(NJ):
                E("dv", lambda e, ib=ib: e.tensor_copy(
                    wbuf[:, ib * 1024:(ib + 1) * 1024], wstage[ib % 3][:]),
                  reads=[b_wstage[ib % 3]], writes=[b_wslot[ib]])
                if ib + 2 < NJ:
                    wload(ib + 2)
                if hook is not None and ib > 0:
                    hook(ib - 1)
                for jb in range(NJ):
                    for tc in range(4):
                        E("pe", lambda e, ib=ib, tc=tc, jb=jb: e.matmul(
                            BK(tc),
                            wbuf[:, ib * 1024 + jb * 128:
                                 ib * 1024 + jb * 128 + 128],
                            xnT[:, jb * S + tc * 512:
                                jb * S + tc * 512 + 512],
                            start=(jb == 0), stop=(jb == NJ - 1)),
                          reads=[b_wslot[ib], b_xnT[(jb, tc)]],
                          acc_first=[b_pb[tc]] if jb == 0 else (),
                          acc_last=[b_pb[tc]] if jb == NJ - 1 else ())
                for tc in range(4):
                    if kscale is None:
                        E("ac", lambda e, ib=ib, tc=tc: e.copy(
                            dest[:, ib * S + tc * 512:
                                 ib * S + tc * 512 + 512],
                            BK(tc)),
                          reads=[b_pb[tc]], writes=[destbufs[(ib, tc)]])
                    else:
                        E("dv", lambda e, ib=ib, tc=tc: e.tensor_scalar(
                            dest[:, ib * S + tc * 512:
                                 ib * S + tc * 512 + 512],
                            BK(tc), kscale, None, ALU.mult),
                          reads=[b_pb[tc]], writes=[destbufs[(ib, tc)]])

        def prepass(p):
            sl = p % 2
            for hh in range(2):
                pr = hh * 64
                E("pe", lambda e, p=p, pr=pr, hh=hh: e.matmul(
                    BK(4 + hh),
                    KT[pr:pr + 64, p * S: p * S + 128],
                    QT[pr:pr + 64, p * S: p * S + 512],
                    start=True, stop=True, tile_position=(pr, 0)),
                  reads=[b_KT[(p, 0)], b_QT[(p, 0)]],
                  acc_first=[b_pb[4 + hh]], acc_last=[b_pb[4 + hh]])
            E("dv", lambda e, sl=sl: e.reduce_max(
                cm_f[sl][:], BK(4), axis=AXX),
              reads=[b_pb[4]], writes=[b_cm_f[sl]])
            E("dv", lambda e, sl=sl: e.reduce_max(
                stat[sl][:, 0:1], BK(5), axis=AXX),
              reads=[b_pb[5]], writes=[b_stat[sl][0]])
            E("dv", lambda e, sl=sl: e.tensor_max(
                cm_f[sl][:], cm_f[sl][:], stat[sl][:, 0:1]),
              reads=[b_cm_f[sl], b_stat[sl][0]], writes=[b_cm_f[sl]])
            E("dv", lambda e, sl=sl: e.tensor_copy(
                cm_b[sl][:], cm_f[sl][:]),
              reads=[b_cm_f[sl]], writes=[b_cm_b[sl]])
            E("pe", lambda e, sl=sl: e.matmul(
                BK(6, r1=1, c1=128), cm_b[sl][:], ident_b[:],
                start=True, stop=True),
              reads=[b_cm_b[sl], b_ident_b],
              acc_first=[b_pb[6]], acc_last=[b_pb[6]])
            E("dv", lambda e, sl=sl: e.reduce_max(
                c_f[sl][:], BK(6, r1=1, c1=128), axis=AXX),
              reads=[b_pb[6]], writes=[b_c_f[sl]])
            E("dv", lambda e, sl=sl: e.tensor_scalar(
                c_b[sl][:], c_f[sl][:], -1.0, None, ALU.mult),
              reads=[b_c_f[sl]], writes=[b_c_b[sl]])
            E("pe", lambda e, sl=sl: e.matmul(
                BK(7, c1=1), ones_row[:], c_b[sl][:],
                start=True, stop=True),
              reads=[b_ones, b_c_b[sl]],
              acc_first=[b_pb[7]], acc_last=[b_pb[7]])
            E("ac", lambda e, p=p: e.copy(
                negc[:, p:p + 1], BK(7, c1=1)),
              reads=[b_pb[7]], writes=[b_negc[p]])

        proj(d_wq, QT, b_QT, None)
        proj(d_wk, KT, b_KT, 0.125, hook=prepass)
        prepass(NH // 2 - 1)
        att_w.close()

        wv_es = ExitStack()
        Ht_a = wv_es.enter_context(
            nc.sbuf_tensor(f"Hta{n}", [128, WHB * 1024], BF16, side="right"))
        wv_es2 = ExitStack()
        wgw = [wv_es2.enter_context(nc.sbuf_tensor(
            f"wgw{n}_{i}", [128, 1024], BF16)) for i in range(2)]
        wuw = [wv_es2.enter_context(nc.sbuf_tensor(
            f"wuw{n}_{i}", [128, 1024], BF16)) for i in range(2)]
        stg_g = wv_es2.enter_context(
            nc.sbuf_tensor(f"stgg{n}", [128, 1024], I32))
        stg_u = wv_es2.enter_context(
            nc.sbuf_tensor(f"stgu{n}", [128, 1024], I32))
        b_wgw = [B(), B()]
        b_wuw = [B(), B()]
        b_stg_g, b_stg_u = B(), B()
        for _b in (b_wgw[0], b_wgw[1], b_wuw[0], b_wuw[1], b_stg_g, b_stg_u):
            _b.r = {"pe": mk.cnt["pe"], "dv": mk.cnt["dv"]}
        t_w = xin[0][:, 512:1024]
        s1_w = xin[1][:, 512:1024]
        wv_state = [0]

        def wv_slot():
            i = wv_state[0]
            if i >= WHB * 4:
                return
            wv_state[0] = i + 1
            hb = i // 4
            sub = i % 4
            tc = sub // 2
            phase = sub % 2
            par = hb % 2
            if sub == 0:
                E("sp", dma=True, fn=lambda e, hb=hb: e.dma_start(
                    out=stg_g[:], in_=d_wg.ap()[hb * 128:(hb + 1) * 128, :]),
                  writes=[b_stg_g])
                E("gp", dma=True, fn=lambda e, hb=hb: e.dma_start(
                    out=stg_u[:], in_=d_wu.ap()[hb * 128:(hb + 1) * 128, :]),
                  writes=[b_stg_u])
                if WV_NODQ:
                    return
                E("dv", lambda e, par=par: e.tensor_scalar(
                    wgw[par][:], stg_g[:], INV127 * (0.5 if WV_TANH else 1.0),
                    None, ALU.mult),
                  reads=[b_stg_g], writes=[b_wgw[par]])
                E("dv", lambda e, par=par: e.tensor_scalar(
                    wuw[par][:], stg_u[:], INV127, None, ALU.mult),
                  reads=[b_stg_u], writes=[b_wuw[par]])
            if WV_NOMM:
                return
            if phase == 0:
                for jb in range(NJ):
                    E("pe", lambda e, par=par, jb=jb, tc=tc: e.matmul(
                        BK(7),
                        wgw[par][:, jb * 128:(jb + 1) * 128],
                        xnT[:, jb * S + tc * 512: jb * S + tc * 512 + 512],
                        start=(jb == 0), stop=(jb == NJ - 1)),
                      reads=[b_wgw[par], b_xnT[(jb, tc)]],
                      acc_first=[b_pb[7]] if jb == 0 else (),
                      acc_last=[b_pb[7]] if jb == NJ - 1 else ())
                if WV_NOACT:
                    E("dv", lambda e: e.tensor_copy(t_w, BK(7)),
                      reads=[b_pb[7]], writes=[b_xin[0]])
                else:
                    E("ac", lambda e: e.activation(
                        out=t_w, in_=BK(7),
                        func=AF.Tanh if WV_TANH else AF.Silu),
                      reads=[b_pb[7]], writes=[b_xin[0]])
            else:
                if WV_TANH:
                    E("dv", lambda e: e.scalar_tensor_tensor(
                        s1_w, t_w, 1.0, BK(7), ALU.add, ALU.mult),
                      reads=[b_xin[0], b_pb[7]], writes=[b_xin[1]])
                else:
                    E("dv", lambda e: e.tensor_copy(s1_w, t_w),
                      reads=[b_xin[0]], writes=[b_xin[1]])
                for jb in range(NJ):
                    E("pe", lambda e, par=par, jb=jb, tc=tc: e.matmul(
                        BK(7),
                        wuw[par][:, jb * 128:(jb + 1) * 128],
                        xnT[:, jb * S + tc * 512: jb * S + tc * 512 + 512],
                        start=(jb == 0), stop=(jb == NJ - 1)),
                      reads=[b_wuw[par], b_xnT[(jb, tc)]],
                      acc_first=[b_pb[7]] if jb == 0 else (),
                      acc_last=[b_pb[7]] if jb == NJ - 1 else ())
                if WV_DISCARD:
                    E("dv", lambda e: e.tensor_tensor(
                        t_w, s1_w, BK(7), ALU.mult),
                      reads=[b_xin[1], b_pb[7]], writes=[b_xin[0]])
                else:
                    E("dv", lambda e, hb=hb, tc=tc: e.tensor_tensor(
                        Ht_a[:, hb * 1024 + tc * 512:
                             hb * 1024 + tc * 512 + 512],
                        s1_w, BK(7), ALU.mult),
                      reads=[b_xin[1], b_pb[7]], writes=[b_H[(hb, tc)]])

        pend = [None]

        def mk_tail_b(p, qc):
            def emit_tail_b():
                for hh in range(2):
                    pr = hh * 64
                    E("pe", lambda e, hh=hh: e.matmul(
                        BK(6, r1=64), ones_row[0:1, 0:64], rc_b[hh][:],
                        start=True, stop=True),
                      reads=[b_ones, b_rc_b[hh]],
                      acc_first=[b_pb[6]], acc_last=[b_pb[6]])
                    E("dv", lambda e, hh=hh, pr=pr: e.tensor_tensor(
                        QT[pr:pr + 64,
                           p * S + qc * 512: p * S + qc * 512 + 512],
                        av_sb[hh][0:64, :], BK(6, r1=64), ALU.mult),
                      reads=[b_av_sb[hh], b_pb[6]], writes=[b_QT[(p, qc)]])
            return emit_tail_b

        for p in range(NH // 2):
            for qc in range(4):
                def sexp(kb):
                    for hh in range(2):
                        pr = hh * 64
                        bk = (kb % 2) * 2 + hh
                        E("pe", lambda e, p=p, pr=pr, kb=kb, qc=qc, bk=bk:
                          e.matmul(
                              BK(bk),
                              KT[pr:pr + 64,
                                 p * S + kb * 128: p * S + kb * 128 + 128],
                              QT[pr:pr + 64,
                                 p * S + qc * 512: p * S + qc * 512 + 512],
                              start=True, stop=True, tile_position=(pr, 0)),
                          reads=[b_KT[(p, kb // 4)], b_QT[(p, qc)]],
                          acc_first=[b_pb[bk]], acc_last=[b_pb[bk]])
                    par = kb % 2
                    slot = kb % 4
                    E("ac", lambda e, slot=slot, p=p, par=par: e.activation(
                        out=expS[:, slot * 1024:(slot + 1) * 1024],
                        in_=pbig[par][0:128, 0:1024], func=AF.Exp,
                        bias=negc[:, p:p + 1]),
                      reads=[b_pb[2 * par], b_pb[2 * par + 1], b_negc[p]],
                      writes=[b_expS[slot]])

                def avk(kb):
                    for hh in range(2):
                        h = 2 * p + hh
                        slot = kb % 4
                        abk = 4 + hh
                        E("pe", lambda e, kb=kb, h=h, hh=hh, slot=slot,
                          abk=abk: e.matmul(
                              BK(abk),
                              Vaug[:, kb * NH * 65 + h * 65:
                                   kb * NH * 65 + h * 65 + 128],
                              expS[:, slot * 1024 + hh * 512:
                                   slot * 1024 + hh * 512 + 512],
                              start=(kb == 0), stop=(kb == NT - 1)),
                          reads=[b_Vaug[kb], b_Vaug[min(kb + 1, NT - 1)],
                                 b_expS[slot]],
                          acc_first=[b_pb[abk]] if kb == 0 else (),
                          acc_last=[b_pb[abk]] if kb == NT - 1 else ())

                ci = p * 4 + qc
                for kb in range(NT):
                    sexp(kb)
                    if kb == 8 and pend[0] is not None:
                        pend[0]()
                        pend[0] = None
                    if WVON and (kb == 5 or (kb == 11 and ci % 2 == 0)):
                        wv_slot()
                    if kb % 2 == 1 and kb >= 3:
                        avk(kb - 3)
                        avk(kb - 2)
                avk(NT - 2)
                avk(NT - 1)
                for hh in range(2):
                    abk = 4 + hh
                    E("dv", lambda e, hh=hh, abk=abk: e.tensor_copy(
                        av_sb[hh][:], BK(abk, r1=65)),
                      reads=[b_pb[abk]], writes=[b_av_sb[hh]])
                    E("dv", lambda e, hh=hh: e.reciprocal(
                        rc_f[hh][:], av_sb[hh][64:65, :]),
                      reads=[b_av_sb[hh]], writes=[b_rc_f[hh]])
                    E("dv", lambda e, hh=hh: e.tensor_copy(
                        rc_b[hh][:], rc_f[hh][:]),
                      reads=[b_rc_f[hh]], writes=[b_rc_b[hh]])
                pend[0] = mk_tail_b(p, qc)
        pend[0]()
        pend[0] = None
        assert wv_state[0] == (WHB * 4 if WVON else 0), wv_state[0]

        wv_es2.close()
        att.close()

        ffn = ExitStack()
        Ht = ffn.enter_context(
            nc.sbuf_tensor(f"H{n}", [128, (NHB - WHB) * 1024], BF16))

        def Ht_sl(hb, c0, c1):
            if hb < WHB:
                return Ht_a[:, hb * 1024 + c0: hb * 1024 + c1]
            return Ht[:, (hb - WHB) * 1024 + c0: (hb - WHB) * 1024 + c1]
        wg_b2 = ffn.enter_context(nc.sbuf_tensor(f"wgb{n}", [128, 2048], BF16))
        wu_b2 = ffn.enter_context(nc.sbuf_tensor(f"wub{n}", [128, 2048], BF16))
        gst = ffn.enter_context(nc.sbuf_tensor(f"gst{n}", [128, 2048], I32))
        ust = ffn.enter_context(nc.sbuf_tensor(f"ust{n}", [128, 2048], I32))
        wd_b = ffn.enter_context(
            nc.sbuf_tensor(f"wdb{n}", [128, NHB * 512], BF16))
        wo_b = ffn.enter_context(
            nc.sbuf_tensor(f"wob{n}", [128, 8 * 512], BF16))
        dst = ffn.enter_context(nc.sbuf_tensor(f"dst{n}_0", [128, 2048], I32))
        ost_g = gst[:].bitcast(F32)
        ost_d = dst[:].bitcast(F32)
        ost_u = ust[:].bitcast(F32)
        b_wg = [B(), B()]
        b_wu = [B(), B()]
        b_gst = B()
        b_ust = B()
        b_wd = [B() for _ in range(NHB)]
        b_wo = [B() for _ in range(8)]
        b_dst = B()


        mk.barrier()
        gs = [xin[0][:, 0:512], xin[1][:, 0:512]]

        for hf in range(2):
            g2_list = (list(range(WHB // 2, NHB // 2))
                       if (hf == 0 and WVON and not WV_DISCARD)
                       else list(range(NHB // 2)))
            for gi, g2 in enumerate(g2_list):
                hb0 = 2 * g2
                E("sp", dma=True, fn=lambda e, hb0=hb0: e.dma_start(
                    out=gst[:],
                    in_=bass.AP(d_wg, hb0 * 128 * 1024,
                                [[1024, 128], [128 * 1024, 2], [1, 1024]])),
                  writes=[b_gst])
                E("dv", lambda e: e.tensor_scalar(
                    wg_b2[:], gst[:], INV127, None, ALU.mult),
                  reads=[b_gst], writes=[b_wg[0], b_wg[1]])
                E("gp", dma=True, fn=lambda e, hb0=hb0: e.dma_start(
                    out=ust[:],
                    in_=bass.AP(d_wu, hb0 * 128 * 1024,
                                [[1024, 128], [128 * 1024, 2], [1, 1024]])),
                  writes=[b_ust])
                E("dv", lambda e: e.tensor_scalar(
                    wu_b2[:], ust[:], INV127, None, ALU.mult),
                  reads=[b_ust], writes=[b_wu[0], b_wu[1]])
                for hb in (hb0, hb0 + 1):
                    hoff = (hb % 2) * 1024
                    for tc in range(2):
                        gtc = hf * 2 + tc
                        gbk, ubk = tc, 2 + tc
                        for jb in range(NJ):
                            E("pe", lambda e, hoff=hoff, jb=jb, gtc=gtc,
                              gbk=gbk: e.matmul(
                                  BK(gbk),
                                  wg_b2[:, hoff + jb * 128:
                                        hoff + (jb + 1) * 128],
                                  xnT[:, jb * S + gtc * 512:
                                      jb * S + gtc * 512 + 512],
                                  start=(jb == 0), stop=(jb == NJ - 1)),
                              reads=[b_wg[hb % 2], b_xnT[(jb, gtc)]],
                              acc_first=[b_pb[gbk]] if jb == 0 else (),
                              acc_last=[b_pb[gbk]] if jb == NJ - 1 else ())
                        for jb in range(NJ):
                            E("pe", lambda e, hoff=hoff, jb=jb, gtc=gtc,
                              ubk=ubk: e.matmul(
                                  BK(ubk),
                                  wu_b2[:, hoff + jb * 128:
                                        hoff + (jb + 1) * 128],
                                  xnT[:, jb * S + gtc * 512:
                                      jb * S + gtc * 512 + 512],
                                  start=(jb == 0), stop=(jb == NJ - 1)),
                              reads=[b_wu[hb % 2], b_xnT[(jb, gtc)]],
                              acc_first=[b_pb[ubk]] if jb == 0 else (),
                              acc_last=[b_pb[ubk]] if jb == NJ - 1 else ())
                        E("ac", lambda e, tc=tc, gbk=gbk: e.activation(
                            out=gs[tc][:], in_=BK(gbk), func=AF.Silu),
                          reads=[b_pb[gbk]], writes=[b_gs[tc], b_xin[tc]])
                        E("dv", lambda e, hb=hb, tc=tc, ubk=ubk: e.tensor_tensor(
                            Ht_sl(hb, tc * 512, tc * 512 + 512),
                            gs[tc][:], BK(ubk), ALU.mult),
                          reads=[b_gs[tc], b_xin[tc], b_pb[ubk]],
                          writes=[b_H[(hb, tc)]])
                short = hf == 0 and WVON and not WV_DISCARD
                if short or gi % 2 == 0:
                    g4 = gi if short else gi // 2
                    E("sp" if g4 % 2 == 0 else "gp", dma=True,
                      fn=lambda e, g4=g4: e.dma_start(
                          out=dst[:],
                          in_=bass.AP(d_wd, g4 * 4 * 128 * 1024,
                                      [[1024, 128], [128 * 1024, 4], [1, 512]])),
                      writes=[b_dst])
                    E("dv", lambda e, g4=g4: e.tensor_scalar(
                        wd_b[:, g4 * 2048:(g4 + 1) * 2048], dst[:],
                        INV127, None, ALU.mult),
                      reads=[b_dst],
                      writes=[b_wd[j] for j in range(4 * g4, 4 * g4 + 4)])

            def wo_ap(g4, dc):
                return bass.AP(d_wo, g4 * 4 * 128 * 1024 + dc * 512,
                               [[1024, 128], [128 * 1024, 4], [1, 512]])

            def wd_ap(g4, dc):
                return bass.AP(d_wd, g4 * 4 * 128 * 1024 + dc * 512,
                               [[1024, 128], [128 * 1024, 4], [1, 512]])

            def wo_cp(dst_half, src_ap, src_buf):
                E("dv", lambda e, dst_half=dst_half, src_ap=src_ap:
                  e.tensor_copy(
                      wo_b[:, dst_half * 2048:(dst_half + 1) * 2048], src_ap),
                  reads=[src_buf],
                  writes=[b_wo[j] for j in range(4 * dst_half,
                                                 4 * dst_half + 4)])

            def wd_dq(g4, stg, stg_buf):
                E("dv", lambda e, g4=g4, stg=stg: e.tensor_scalar(
                    wd_b[:, g4 * 2048:(g4 + 1) * 2048], stg[:],
                    INV127, None, ALU.mult),
                  reads=[stg_buf],
                  writes=[b_wd[j] for j in range(4 * g4, 4 * g4 + 4)])

            wd_rot = [(dst, b_dst), (ust, b_ust), (gst, b_gst)]

            for dc in range(2):
                if dc == 0:
                    E("sp", dma=True, fn=lambda e: e.dma_start(
                        out=ost_g[:], in_=wo_ap(0, 0)), writes=[b_gst])
                    E("gp", dma=True, fn=lambda e: e.dma_start(
                        out=ost_d[:], in_=wo_ap(1, 0)), writes=[b_dst])
                    wo_cp(0, ost_g[:], b_gst)
                    wo_cp(1, ost_d[:], b_dst)
                    E("sp", dma=True, fn=lambda e: e.dma_start(
                        out=ost_d[:], in_=wo_ap(0, 1)), writes=[b_dst])
                    E("gp", dma=True, fn=lambda e: e.dma_start(
                        out=ost_u[:], in_=wo_ap(1, 1)), writes=[b_ust])
                    E("sp", dma=True, fn=lambda e: e.dma_start(
                        out=gst[:], in_=wd_ap(0, 1)), writes=[b_gst])
                else:
                    wo_cp(0, ost_d[:], b_dst)
                    E("sp", dma=True, fn=lambda e: e.dma_start(
                        out=dst[:], in_=wd_ap(1, 1)), writes=[b_dst])
                    wo_cp(1, ost_u[:], b_ust)
                    E("gp", dma=True, fn=lambda e: e.dma_start(
                        out=ust[:], in_=wd_ap(2, 1)), writes=[b_ust])
                    wd_dq(0, gst, b_gst)
                    E("sp", dma=True, fn=lambda e: e.dma_start(
                        out=gst[:], in_=wd_ap(3, 1)), writes=[b_gst])
                    wd_dq(1, dst, b_dst)
                    E("gp", dma=True, fn=lambda e: e.dma_start(
                        out=dst[:], in_=wd_ap(4, 1)), writes=[b_dst])
                    wd_dq(2, ust, b_ust)
                    E("sp", dma=True, fn=lambda e: e.dma_start(
                        out=ust[:], in_=wd_ap(5, 1)), writes=[b_ust])
                    wd_dq(3, gst, b_gst)
                    wd_dq(4, dst, b_dst)
                    wd_dq(5, ust, b_ust)
                for ti in range(8):
                    if hf == 1 and weave_next is not None:
                        weave_next(dc * 8 + ti)
                    tb = hf * 8 + ti
                    xi = ti % 2
                    oi = (dc * 8 + ti) % 3
                    obk = 4 + (dc * 8 + ti) % 2
                    E("ac", dma=True, fn=lambda e, tb=tb, dc=dc, xi=xi: e.dma_start(
                        out=xr[xi][:],
                        in_=d_x.ap()[n * S + tb * 128: n * S + tb * 128 + 128,
                                     dc * 512:(dc + 1) * 512]),
                      writes=[b_xr[xi]])
                    E("pe", lambda e, dc=dc, obk=obk: e.matmul(
                        BK(obk), ones_row[:],
                        bo_b[0:1, dc * 512:(dc + 1) * 512],
                        start=True, stop=False),
                      reads=[b_ones, b_bo_b],
                      acc_first=[b_pb[obk]])

                    def wo_mms(obk=obk, tb=tb, last=False):
                        for jb in range(NJ):
                            E("pe", lambda e, jb=jb, tb=tb, obk=obk,
                              last=last: e.matmul(
                                  BK(obk),
                                  QT[:, jb * S + tb * 128:
                                     jb * S + tb * 128 + 128],
                                  wo_b[:, jb * 512:(jb + 1) * 512],
                                  start=False,
                                  stop=(last and jb == NJ - 1)),
                              reads=[b_QT[(jb, tb // 4)], b_wo[jb]],
                              acc_last=[b_pb[obk]]
                              if (last and jb == NJ - 1) else ())

                    def wd_mms(obk=obk, ti=ti, last=False):
                        for hb in range(NHB):
                            E("pe", lambda e, hb=hb, ti=ti, obk=obk,
                              last=last: e.matmul(
                                  BK(obk),
                                  Ht_sl(hb, ti * 128, ti * 128 + 128),
                                  wd_b[:, hb * 512:(hb + 1) * 512],
                                  start=False,
                                  stop=(last and hb == NHB - 1)),
                              reads=[b_H[(hb, ti // 4)], b_wd[hb]],
                              acc_last=[b_pb[obk]]
                              if (last and hb == NHB - 1) else ())

                    if dc == 0:
                        wd_mms(last=False)
                        wo_mms(last=True)
                    else:
                        wo_mms(last=False)
                        wd_mms(last=True)
                    E("dv", lambda e, xi=xi, oi=oi, obk=obk: e.tensor_add(
                        out4[oi][:], xr[xi][:], BK(obk)),
                      reads=[b_xr[xi], b_pb[obk]], writes=[b_out4[oi]])
                    E("gp", dma=True, fn=lambda e, tb=tb, dc=dc, oi=oi: e.dma_start(
                        out=d_y.ap()[n * S + tb * 128: n * S + tb * 128 + 128,
                                     dc * 512:(dc + 1) * 512],
                        in_=out4[oi][:]),
                      reads=[b_out4[oi]])

        if weave_next is not None:
            weave_next(NT)
            weave_next(NT + 1)
        ffn.close()
        wv_es.close()

    ln1_slot_fn, _ = make_ln(1)

    emit_elem(0, weave_next=ln1_slot_fn)
    emit_elem(1, own_ln=False)
    mk.final_wait("gp")

    with nc.Block() as block:
        def section(eng_name):
            def body(EO):
                wm = {}
                for waits, fn, inc in mk.th[eng_name]:
                    for s in sorted(waits):
                        v = waits[s]
                        if wm.get(s, 0) < v:
                            EO.wait_ge(sems[s], v)
                            wm[s] = v
                    if fn is not None:
                        fn(EO).then_inc(sems[inc[0]], inc[1])
            return body

        block.tensor(section("pe"))
        block.vector(section("dv"))
        block.scalar(section("ac"))
        block.sync(section("sp"))
        block.gpsimd(section("gp"))

    es.close()
    return nc


_NC_CACHE = None


def _get_nc():
    global _NC_CACHE
    if _NC_CACHE is None:
        _NC_CACHE = build_program()
    return _NC_CACHE


def kernel(L0_attn_x, L0_attn_Wq, L0_attn_Wk, L0_attn_Wv, L0_attn_Wo,
           L0_attn_bo, L0_ffn_w_gate, L0_ffn_w_up, L0_ffn_w_down):
    x = np.ascontiguousarray(np.asarray(L0_attn_x, dtype=np.float32))
    wq = np.asarray(L0_attn_Wq, dtype=np.float32)
    wk = np.asarray(L0_attn_Wk, dtype=np.float32)
    wv = np.asarray(L0_attn_Wv, dtype=np.float32)
    wo = np.asarray(L0_attn_Wo, dtype=np.float32)
    bo = np.ascontiguousarray(
        np.asarray(L0_attn_bo, dtype=np.float32)).reshape(1, D)
    wg = np.asarray(L0_ffn_w_gate, dtype=np.int32)
    wu = np.asarray(L0_ffn_w_up, dtype=np.int32)
    wd = np.asarray(L0_ffn_w_down, dtype=np.int32)

    def qk_layout(w):
        return np.ascontiguousarray(
            w.reshape(NJ, 128, NJ, 128).transpose(0, 3, 2, 1).reshape(D, D))

    def gu_layout(w):
        return np.ascontiguousarray(
            w.reshape(NHB, 128, NJ, 128).transpose(0, 3, 2, 1).reshape(FH, D))

    wq_h = qk_layout(wq)
    wk_h = qk_layout(wk)
    wv_h = np.ascontiguousarray(wv.T)
    wo_h = np.ascontiguousarray(wo.T)
    wg_h = gu_layout(wg)
    wu_h = gu_layout(wu)
    wd_h = np.ascontiguousarray(wd.T)
    ident = np.eye(128, dtype=np.float32)

    nc = _get_nc()
    in_maps = []
    for c in range(8):
        in_maps.append({
            "x": np.ascontiguousarray(x[2 * c: 2 * c + 2].reshape(NB * S, D)),
            "wq": wq_h, "wk": wk_h, "wv": wv_h, "wo": wo_h, "bo": bo,
            "wg": wg_h, "wu": wu_h, "wd": wd_h, "ident": ident,
        })
    res = run_bass_kernel_spmd(nc, in_maps, core_ids=list(range(8)))
    out = np.empty((16, S, D), dtype=np.float32)
    for c in range(8):
        out[2 * c: 2 * c + 2] = np.asarray(
            res.results[c]["y"]).reshape(NB, S, D)
    return out


# version 21
# speedup vs baseline: 1.2700x; 1.2700x over previous
import numpy as np
from contextlib import ExitStack
import concourse.bass as bass
import concourse.mybir as mybir
from concourse.bass_utils import run_bass_kernel_spmd

F32 = mybir.dt.float32
BF16 = mybir.dt.bfloat16
I32 = mybir.dt.int32
AF = mybir.ActivationFunctionType
AXX = mybir.AxisListType.X
ALU = mybir.AluOpType

D = 1024
S = 2048
NB = 2
NH = 16
HD = 64
FH = 3072
NT = S // 128
NJ = D // 128
NHB = FH // 128
EPS = 1e-5
INV127 = float(np.float32(1.0) / np.float32(127.0))


class Buf:
    __slots__ = ("w", "r")

    def __init__(self):
        self.w = None
        self.r = {}


class MK:
    ENG = ("pe", "dv", "ac", "sp", "gp")
    NLANES = {"sp": 8, "gp": 4, "ac": 2}

    def __init__(self):
        self.th = {e: [] for e in self.ENG}
        self.cnt = {e: 0 for e in self.ENG}
        for q, nl in self.NLANES.items():
            for l in range(nl):
                self.cnt[f"{q}{l}"] = 0
        self.dmacnt = {"sp": 0, "gp": 0, "ac": 0}
        self.pending = {}

    def all_sems(self):
        return list(self.cnt)

    def emit(self, eng, fn=None, reads=(), writes=(), acc_first=(), acc_last=(),
             extra=(), dma=False):
        waits = {}

        def need(tok):
            if tok is not None:
                s, v = tok
                if waits.get(s, 0) < v:
                    waits[s] = v

        for b in reads:
            need(b.w)
        for b in tuple(writes) + tuple(acc_first):
            need(b.w)
            for s, v in b.r.items():
                need((s, v))
        for t in extra:
            need(t)
        p = self.pending.pop(eng, None)
        if p:
            for s, v in p.items():
                need((s, v))
        if dma:
            i = self.dmacnt[eng]
            self.dmacnt[eng] = i + 1
            lane = i % self.NLANES[eng]
            semn = f"{eng}{lane}"
            if self.cnt[semn] > 0:
                need((semn, self.cnt[semn]))
            self.cnt[semn] += 16
            tok = (semn, self.cnt[semn])
            inc = (semn, 16)
        else:
            self.cnt[eng] += 1
            tok = (eng, self.cnt[eng])
            inc = (eng, 1)
        for b in reads:
            if b.r.get(tok[0], 0) < tok[1]:
                b.r[tok[0]] = tok[1]
        for b in tuple(writes) + tuple(acc_last):
            b.w = tok
            b.r = {}
        self.th[eng].append((waits, fn, inc))
        return tok

    def barrier(self):
        caps = {s: v for s, v in self.cnt.items() if v > 0}
        for e in self.ENG:
            p = self.pending.setdefault(e, {})
            for s, v in caps.items():
                if p.get(s, 0) < v:
                    p[s] = v

    def final_wait(self, eng):
        cap = dict(self.cnt)
        self.th[eng].append(
            ({s: v for s, v in cap.items() if v > 0}, None, None))


def build_program():
    nc = bass.Bass("TRN2", target_bir_lowering=False, monotonic_sem_count=0)
    d_x = nc.dram_tensor("x", [NB * S, D], F32, kind="ExternalInput")
    d_wq = nc.dram_tensor("wq", [D, D], F32, kind="ExternalInput")
    d_wk = nc.dram_tensor("wk", [D, D], F32, kind="ExternalInput")
    d_wv = nc.dram_tensor("wv", [D, D], F32, kind="ExternalInput")
    d_wo = nc.dram_tensor("wo", [D, D], F32, kind="ExternalInput")
    d_bo = nc.dram_tensor("bo", [1, D], F32, kind="ExternalInput")
    d_wg = nc.dram_tensor("wg", [FH, D], I32, kind="ExternalInput")
    d_wu = nc.dram_tensor("wu", [FH, D], I32, kind="ExternalInput")
    d_wd = nc.dram_tensor("wd", [FH, D], I32, kind="ExternalInput")
    d_id = nc.dram_tensor("ident", [128, 128], F32, kind="ExternalInput")
    d_y = nc.dram_tensor("y", [NB * S, D], F32, kind="ExternalOutput")

    es = ExitStack()
    SB = lambda nm, sh, dt: es.enter_context(nc.sbuf_tensor(nm, sh, dt))
    mk = MK()
    E = mk.emit

    xnT = SB("xnT", [128, NJ * S], BF16)
    QT = SB("QT", [128, NJ * S], BF16)
    xin = [SB(f"xin{i}", [128, D], F32) for i in range(3)]
    xr = [SB(f"xr{i}", [128, 512], F32) for i in range(2)]
    out4 = [SB(f"out{i}", [128, 512], F32) for i in range(3)]
    ident_b = SB("idb", [128, 128], BF16)
    ones_row = SB("ones", [1, 128], BF16)
    bo_b = SB("bob", [1, D], BF16)
    negc = SB("negc", [128, NH], F32)
    stat = [SB(f"stat{i}", [128, 16], F32) for i in range(2)]
    xnb = [SB(f"xnb{i}", [128, D], BF16) for i in range(2)]
    gs = None
    cm_f = [SB(f"cmf{i}", [128, 1], F32) for i in range(2)]
    cm_b = [SB(f"cmb{i}", [128, 1], BF16) for i in range(2)]
    c_f = [SB(f"cf{i}", [1, 1], F32) for i in range(2)]
    c_b = [SB(f"cb{i}", [1, 1], BF16) for i in range(2)]
    rc_b = [SB(f"rcb{i}", [1, 512], BF16) for i in range(2)]
    rc_f = [out4[0][0:1, :], out4[1][0:1, :]]
    b_rc_f = [None, None]
    pbig = [es.enter_context(nc.psum_tensor(f"pbig{i}", [128, 1024], F32))
            for i in range(2)]
    psm = [es.enter_context(nc.psum_tensor(f"psm{i}", [128, 512], F32))
           for i in range(4)]

    def BK(k, r0=0, r1=128, c0=0, c1=512):
        if k < 4:
            t = pbig[k // 2]
            off = (k % 2) * 512
            return t[r0:r1, off + c0: off + c1]
        return psm[k - 4][r0:r1, c0:c1]
    sems = {s: es.enter_context(nc.semaphore(f"sem_{s}")) for s in mk.all_sems()}

    B = Buf
    b_xin = [B(), B(), B()]
    b_stat = [[B() for _ in range(8)] for _ in range(2)]
    b_xnb = [B(), B()]
    b_xnT = {(j, tc): B() for j in range(NJ) for tc in range(4)}
    b_QT = {(j, qc): B() for j in range(NJ) for qc in range(4)}
    b_pb = [B() for _ in range(8)]
    b_ident_f, b_ident_b, b_ones = B(), B(), B()
    b_bo_f, b_bo_b = B(), B()
    b_negc = [B() for _ in range(NH)]
    b_gs = [B(), B()]
    b_cm_f = [B(), B()]
    b_cm_b = [B(), B()]
    b_c_f = [B(), B()]
    b_c_b = [B(), B()]
    b_rc_b = [B(), B()]
    b_av_sb = [B(), B()]
    b_xr = [B(), B()]
    b_out4 = [B() for _ in range(3)]
    b_rc_f[0] = b_out4[0]
    b_rc_f[1] = b_out4[1]

    E("sp", dma=True, fn=lambda e: e.dma_start(out=ident_f, in_=d_id.ap()),
      writes=[b_ident_f, b_xin[1]])
    bo_f = xin[0][0:1, :]
    ident_f = xin[1][:, 0:128]
    E("sp", dma=True, fn=lambda e: e.dma_start(out=bo_f, in_=d_bo.ap()),
      writes=[b_bo_f, b_xin[0]])
    E("dv", lambda e: e.tensor_copy(ident_b[:], ident_f),
      reads=[b_ident_f, b_xin[1]], writes=[b_ident_b])
    E("dv", lambda e: e.memset(ones_row[:], 1.0), writes=[b_ones])
    E("dv", lambda e: e.tensor_copy(bo_b[:], bo_f),
      reads=[b_bo_f, b_xin[0]], writes=[b_bo_b])

    ROW = NJ * S

    def make_ln(n):
            def ln_s1(tb):
                i = tb % 2
                xi = tb % 3
                st_ = b_stat[i]
                E("sp", dma=True, fn=lambda e, xi=xi, tb=tb: e.dma_start(
                    out=xin[xi][:],
                    in_=d_x.ap()[n * S + tb * 128: n * S + tb * 128 + 128, :]),
                  writes=[b_xin[xi]])
                E("dv", lambda e, i=i, xi=xi: e.bn_stats(
                    stat[i][:, 0:6], xin[xi][:, 0:512]),
                  reads=[b_xin[xi]], writes=[st_[0]])
                E("dv", lambda e, i=i, xi=xi: e.bn_stats(
                    stat[i][:, 6:12], xin[xi][:, 512:1024]),
                  reads=[b_xin[xi]], writes=[st_[1]])
                E("dv", lambda e, i=i: e.bn_aggr(
                    stat[i][:, 12:14], stat[i][:, 0:12]),
                  reads=[st_[0], st_[1]], writes=[st_[2]])
                E("dv", lambda e, i=i: e.tensor_scalar(
                    stat[i][:, 13:14], stat[i][:, 13:14], EPS, None, ALU.add),
                  reads=[st_[2]], writes=[st_[3]])

            def ln_s2(tb):
                i = tb % 2
                xi = tb % 3
                st_ = b_stat[i]
                E("ac", lambda e, i=i: e.activation(
                    out=stat[i][:, 14:15], in_=stat[i][:, 13:14], func=AF.Sqrt),
                  reads=[st_[3]], writes=[st_[4]])
                E("dv", lambda e, i=i: e.reciprocal(
                    stat[i][:, 15:16], stat[i][:, 14:15]),
                  reads=[st_[4]], writes=[st_[5]])
                E("dv", lambda e, i=i: e.scalar_tensor_tensor(
                    stat[i][:, 14:15], stat[i][:, 12:13], -1.0, stat[i][:, 15:16],
                    ALU.mult, ALU.mult),
                  reads=[st_[2], st_[5]], writes=[st_[6]])
                E("ac", lambda e, i=i, xi=xi: e.activation(
                    out=xnb[i][:], in_=xin[xi][:], func=AF.Identity,
                    bias=stat[i][:, 14:15], scale=stat[i][:, 15:16]),
                  reads=[b_xin[xi], st_[5], st_[6]], writes=[b_xnb[i]])

            def ln_s3(tb):
                i = tb % 2
                bA, bB = 6, 7
                for half, bk in ((0, bA), (1, bB)):
                    for q in range(4):
                        db = half * 4 + q
                        E("pe", lambda e, i=i, db=db, bk=bk, q=q: e.matmul(
                            BK(bk, c0=q * 128, c1=(q + 1) * 128),
                            xnb[i][:, db * 128:(db + 1) * 128], ident_b[:],
                            start=True, stop=True),
                          reads=[b_xnb[i], b_ident_b],
                          acc_first=[b_pb[bk]] if q == 0 else (),
                          acc_last=[b_pb[bk]] if q == 3 else ())
                    eng = "dv" if half == 0 else "ac"
                    if eng == "dv":
                        E("dv", lambda e, tb=tb, half=half, bk=bk: e.tensor_copy(
                            bass.AP(xnT, half * 4 * S + tb * 128,
                                    [[ROW, 128], [S, 4], [1, 128]]),
                            BK(bk)),
                          reads=[b_pb[bk]],
                          writes=[b_xnT[(half * 4 + d, tb // 4)] for d in range(4)])
                    else:
                        E("ac", lambda e, tb=tb, half=half, bk=bk: e.copy(
                            bass.AP(xnT, half * 4 * S + tb * 128,
                                    [[ROW, 128], [S, 4], [1, 128]]),
                            BK(bk)),
                          reads=[b_pb[bk]],
                          writes=[b_xnT[(half * 4 + d, tb // 4)] for d in range(4)])

            def ln_slot(tb, vhook=None):
                if tb < NT:
                    ln_s1(tb)
                if tb >= 1 and tb - 1 < NT:
                    ln_s2(tb - 1)
                if tb >= 2 and tb - 2 < NT:
                    ln_s3(tb - 2)
                    if vhook is not None:
                        vhook(tb - 2)

            def ln_pipeline(vhook=None):
                for tb in range(NT + 2):
                    ln_slot(tb, vhook)

            return ln_slot, ln_pipeline

    def emit_elem(n, weave_next=None, own_ln=True):
        ln_slot, ln_pipeline = make_ln(n)
        att = ExitStack()
        KT = att.enter_context(nc.sbuf_tensor(f"KT{n}", [128, NJ * S], BF16))
        Vaug = att.enter_context(
            nc.sbuf_tensor(f"Vaug{n}", [128, NT * NH * 65 + 64], BF16))
        wbuf = att.enter_context(
            nc.sbuf_tensor(f"wbuf{n}", [128, 8 * 1024], BF16))
        wstage = [att.enter_context(
            nc.sbuf_tensor(f"wst{n}_{i}", [128, 1024], F32)) for i in range(3)]
        expS = att.enter_context(
            nc.sbuf_tensor(f"expS{n}", [128, 4 * 1024], BF16))
        av_sb = [att.enter_context(
            nc.sbuf_tensor(f"avsb{n}_{i}", [65, 512], F32)) for i in range(2)]
        b_KT = {(j, tc): B() for j in range(NJ) for tc in range(4)}
        b_Vaug = [B() for _ in range(NT)]
        b_wslot = [B() for _ in range(8)]
        b_wstage = [B(), B(), B()]
        b_expS = [B() for _ in range(4)]
        VROW = NT * NH * 65
        VPITCH = VROW + 64

        def att_scope_open():
            E("dv", lambda e: e.memset(
                bass.AP(Vaug, 64, [[VPITCH, 128], [65, NT * NH]]), 1.0),
              writes=b_Vaug)
            E("dv", lambda e: e.memset(Vaug[:, VROW:VROW + 64], 0.0),
              writes=[b_Vaug[NT - 1]])

            def vload(jb):
                E("sp", dma=True, fn=lambda e, jb=jb: e.dma_start(
                    out=wstage[jb % 3][:],
                    in_=d_wv.ap()[jb * 128:(jb + 1) * 128, :]),
                  writes=[b_wstage[jb % 3]])
            vload(0)
            vload(1)
            for jb in range(NJ):
                E("dv", lambda e, jb=jb: e.tensor_copy(
                    wbuf[:, jb * 1024:(jb + 1) * 1024], wstage[jb % 3][:]),
                  reads=[b_wstage[jb % 3]], writes=[b_wslot[jb]])
                if jb + 2 < NJ:
                    vload(jb + 2)

        def v_pass_tb(tb):
            for dc in range(2):
                bk = (tb * 2 + dc) % 2
                for jb in range(NJ):
                    E("pe", lambda e, tb=tb, dc=dc, jb=jb, bk=bk: e.matmul(
                        BK(bk),
                        xnT[:, jb * S + tb * 128: jb * S + tb * 128 + 128],
                        wbuf[:, jb * 1024 + dc * 512:
                             jb * 1024 + dc * 512 + 512],
                        start=(jb == 0), stop=(jb == NJ - 1)),
                      reads=[b_wslot[jb], b_xnT[(jb, tb // 4)]],
                      acc_first=[b_pb[bk]] if jb == 0 else (),
                      acc_last=[b_pb[bk]] if jb == NJ - 1 else ())
                E("dv", lambda e, tb=tb, dc=dc, bk=bk: e.tensor_copy(
                    bass.AP(Vaug, tb * NH * 65 + dc * 8 * 65,
                            [[VPITCH, 128], [65, 8], [1, 64]]),
                    BK(bk)),
                  reads=[b_pb[bk]], writes=[b_Vaug[tb]])

        if n == 0:
            mk.barrier()
            att_scope_open()
            ln_pipeline(vhook=v_pass_tb)
        else:
            if own_ln:
                ln_pipeline(None)
            mk.barrier()
            att_scope_open()
            for tb in range(NT):
                v_pass_tb(tb)

        def proj(dram, dest, destbufs, kscale, hook=None):
            def wload(ib):
                E("sp", dma=True, fn=lambda e, ib=ib: e.dma_start(
                    out=wstage[ib % 3][:],
                    in_=dram.ap()[ib * 128:(ib + 1) * 128, :]),
                  writes=[b_wstage[ib % 3]])
            wload(0)
            wload(1)
            for ib in range(NJ):
                E("dv", lambda e, ib=ib: e.tensor_copy(
                    wbuf[:, ib * 1024:(ib + 1) * 1024], wstage[ib % 3][:]),
                  reads=[b_wstage[ib % 3]], writes=[b_wslot[ib]])
                if ib + 2 < NJ:
                    wload(ib + 2)
                if hook is not None and ib > 0:
                    hook(ib - 1)
                for jb in range(NJ):
                    for tc in range(4):
                        E("pe", lambda e, ib=ib, tc=tc, jb=jb: e.matmul(
                            BK(tc),
                            wbuf[:, ib * 1024 + jb * 128:
                                 ib * 1024 + jb * 128 + 128],
                            xnT[:, jb * S + tc * 512:
                                jb * S + tc * 512 + 512],
                            start=(jb == 0), stop=(jb == NJ - 1)),
                          reads=[b_wslot[ib], b_xnT[(jb, tc)]],
                          acc_first=[b_pb[tc]] if jb == 0 else (),
                          acc_last=[b_pb[tc]] if jb == NJ - 1 else ())
                for tc in range(4):
                    if kscale is None:
                        E("ac", lambda e, ib=ib, tc=tc: e.copy(
                            dest[:, ib * S + tc * 512:
                                 ib * S + tc * 512 + 512],
                            BK(tc)),
                          reads=[b_pb[tc]], writes=[destbufs[(ib, tc)]])
                    else:
                        E("dv", lambda e, ib=ib, tc=tc: e.tensor_scalar(
                            dest[:, ib * S + tc * 512:
                                 ib * S + tc * 512 + 512],
                            BK(tc), kscale, None, ALU.mult),
                          reads=[b_pb[tc]], writes=[destbufs[(ib, tc)]])

        def prepass(p):
            sl = p % 2
            for hh in range(2):
                pr = hh * 64
                E("pe", lambda e, p=p, pr=pr, hh=hh: e.matmul(
                    BK(4 + hh),
                    KT[pr:pr + 64, p * S: p * S + 128],
                    QT[pr:pr + 64, p * S: p * S + 512],
                    start=True, stop=True, tile_position=(pr, 0)),
                  reads=[b_KT[(p, 0)], b_QT[(p, 0)]],
                  acc_first=[b_pb[4 + hh]], acc_last=[b_pb[4 + hh]])
            E("dv", lambda e, sl=sl: e.reduce_max(
                cm_f[sl][:], BK(4), axis=AXX),
              reads=[b_pb[4]], writes=[b_cm_f[sl]])
            E("dv", lambda e, sl=sl: e.reduce_max(
                stat[sl][:, 0:1], BK(5), axis=AXX),
              reads=[b_pb[5]], writes=[b_stat[sl][0]])
            E("dv", lambda e, sl=sl: e.tensor_max(
                cm_f[sl][:], cm_f[sl][:], stat[sl][:, 0:1]),
              reads=[b_cm_f[sl], b_stat[sl][0]], writes=[b_cm_f[sl]])
            E("dv", lambda e, sl=sl: e.tensor_copy(
                cm_b[sl][:], cm_f[sl][:]),
              reads=[b_cm_f[sl]], writes=[b_cm_b[sl]])
            E("pe", lambda e, sl=sl: e.matmul(
                BK(6, r1=1, c1=128), cm_b[sl][:], ident_b[:],
                start=True, stop=True),
              reads=[b_cm_b[sl], b_ident_b],
              acc_first=[b_pb[6]], acc_last=[b_pb[6]])
            E("dv", lambda e, sl=sl: e.reduce_max(
                c_f[sl][:], BK(6, r1=1, c1=128), axis=AXX),
              reads=[b_pb[6]], writes=[b_c_f[sl]])
            E("dv", lambda e, sl=sl: e.tensor_scalar(
                c_b[sl][:], c_f[sl][:], -1.0, None, ALU.mult),
              reads=[b_c_f[sl]], writes=[b_c_b[sl]])
            E("pe", lambda e, sl=sl: e.matmul(
                BK(7, c1=1), ones_row[:], c_b[sl][:],
                start=True, stop=True),
              reads=[b_ones, b_c_b[sl]],
              acc_first=[b_pb[7]], acc_last=[b_pb[7]])
            E("ac", lambda e, p=p: e.copy(
                negc[:, p:p + 1], BK(7, c1=1)),
              reads=[b_pb[7]], writes=[b_negc[p]])

        proj(d_wq, QT, b_QT, None)
        proj(d_wk, KT, b_KT, 0.125, hook=prepass)
        prepass(NH // 2 - 1)

        pend = [None]

        def mk_tail_b(p, qc):
            def emit_tail_b():
                for hh in range(2):
                    pr = hh * 64
                    bck = 6 + hh
                    E("pe", lambda e, hh=hh, bck=bck: e.matmul(
                        BK(bck, r1=64), ones_row[0:1, 0:64], rc_b[hh][:],
                        start=True, stop=True),
                      reads=[b_ones, b_rc_b[hh]],
                      acc_first=[b_pb[bck]], acc_last=[b_pb[bck]])
                    E("dv", lambda e, hh=hh, pr=pr, bck=bck: e.tensor_tensor(
                        QT[pr:pr + 64,
                           p * S + qc * 512: p * S + qc * 512 + 512],
                        av_sb[hh][0:64, :], BK(bck, r1=64), ALU.mult),
                      reads=[b_av_sb[hh], b_pb[bck]], writes=[b_QT[(p, qc)]])
            return emit_tail_b

        def mk_chunk(p, qc):
            def sexp(kb):
                for hh in range(2):
                    pr = hh * 64
                    bk = (kb % 2) * 2 + hh
                    E("pe", lambda e, p=p, pr=pr, kb=kb, qc=qc, bk=bk:
                      e.matmul(
                          BK(bk),
                          KT[pr:pr + 64,
                             p * S + kb * 128: p * S + kb * 128 + 128],
                          QT[pr:pr + 64,
                             p * S + qc * 512: p * S + qc * 512 + 512],
                          start=True, stop=True, tile_position=(pr, 0)),
                      reads=[b_KT[(p, kb // 4)], b_QT[(p, qc)]],
                      acc_first=[b_pb[bk]], acc_last=[b_pb[bk]])
                par = kb % 2
                slot = kb % 4
                E("ac", lambda e, slot=slot, p=p, par=par: e.activation(
                    out=expS[:, slot * 1024:(slot + 1) * 1024],
                    in_=pbig[par][0:128, 0:1024], func=AF.Exp,
                    bias=negc[:, p:p + 1]),
                  reads=[b_pb[2 * par], b_pb[2 * par + 1], b_negc[p]],
                  writes=[b_expS[slot]])

            def avk(kb):
                for hh in range(2):
                    h = 2 * p + hh
                    slot = kb % 4
                    abk = 4 + hh
                    E("pe", lambda e, kb=kb, h=h, hh=hh, slot=slot,
                      abk=abk: e.matmul(
                          BK(abk),
                          Vaug[:, kb * NH * 65 + h * 65:
                               kb * NH * 65 + h * 65 + 128],
                          expS[:, slot * 1024 + hh * 512:
                               slot * 1024 + hh * 512 + 512],
                          start=(kb == 0), stop=(kb == NT - 1)),
                      reads=[b_Vaug[kb], b_Vaug[min(kb + 1, NT - 1)],
                             b_expS[slot]],
                      acc_first=[b_pb[abk]] if kb == 0 else (),
                      acc_last=[b_pb[abk]] if kb == NT - 1 else ())
            return sexp, avk

        chunks = [(p, qc) for p in range(NH // 2) for qc in range(4)]
        cfs = [mk_chunk(p, qc) for p, qc in chunks]
        for i, (p, qc) in enumerate(chunks):
            sexp, avk = cfs[i]
            for kb in range(1 if i > 0 else 0, NT):
                sexp(kb)
                if kb == 8 and pend[0] is not None:
                    pend[0]()
                    pend[0] = None
                if kb % 2 == 1 and kb >= 3:
                    avk(kb - 3)
                    avk(kb - 2)
                if kb == NT - 1 and i + 1 < len(chunks):
                    cfs[i + 1][0](0)
            avk(NT - 2)
            avk(NT - 1)
            for hh in range(2):
                abk = 4 + hh
                E("dv", lambda e, hh=hh, abk=abk: e.tensor_copy(
                    av_sb[hh][:], BK(abk, r1=65)),
                  reads=[b_pb[abk]], writes=[b_av_sb[hh]])
                E("dv", lambda e, hh=hh: e.reciprocal(
                    rc_f[hh][:], av_sb[hh][64:65, :]),
                  reads=[b_av_sb[hh]], writes=[b_rc_f[hh]])
                E("dv", lambda e, hh=hh: e.tensor_copy(
                    rc_b[hh][:], rc_f[hh][:]),
                  reads=[b_rc_f[hh]], writes=[b_rc_b[hh]])
            pend[0] = mk_tail_b(p, qc)
        pend[0]()
        pend[0] = None

        att.close()

        ffn = ExitStack()
        Ht = ffn.enter_context(
            nc.sbuf_tensor(f"H{n}", [128, NHB * 1024], BF16))
        wg_b2 = ffn.enter_context(nc.sbuf_tensor(f"wgb{n}", [128, 2048], BF16))
        wu_b2 = ffn.enter_context(nc.sbuf_tensor(f"wub{n}", [128, 2048], BF16))
        gst = ffn.enter_context(nc.sbuf_tensor(f"gst{n}", [128, 2048], I32))
        ust = ffn.enter_context(nc.sbuf_tensor(f"ust{n}", [128, 2048], I32))
        wd_b = ffn.enter_context(
            nc.sbuf_tensor(f"wdb{n}", [128, NHB * 512], BF16))
        wo_b = ffn.enter_context(
            nc.sbuf_tensor(f"wob{n}", [128, 8 * 512], BF16))
        dst = ffn.enter_context(nc.sbuf_tensor(f"dst{n}_0", [128, 2048], I32))
        ost_g = gst[:].bitcast(F32)
        ost_d = dst[:].bitcast(F32)
        ost_u = ust[:].bitcast(F32)
        b_H = {(hb, tc): B() for hb in range(NHB) for tc in range(2)}
        b_wg = [B(), B()]
        b_wu = [B(), B()]
        b_gst = B()
        b_ust = B()
        b_wd = [B() for _ in range(NHB)]
        b_wo = [B() for _ in range(8)]
        b_dst = B()


        E("sp", dma=True, fn=lambda e: e.dma_start(
            out=xin[0][:].bitcast(I32), in_=d_wg.ap()[0:128, :]),
          writes=[b_xin[0]])
        E("gp", dma=True, fn=lambda e: e.dma_start(
            out=xin[1][:].bitcast(I32), in_=d_wu.ap()[0:128, :]),
          writes=[b_xin[1]])
        mk.barrier()
        gs = [xin[0][:, 0:512], xin[1][:, 0:512]]

        for hf in range(2):
            for g2 in range(NHB // 2):
                hb0 = 2 * g2
                if hf == 0 and g2 == 0:
                    E("sp", dma=True, fn=lambda e: e.dma_start(
                        out=gst[:, 1024:2048],
                        in_=d_wg.ap()[128:256, :]),
                      writes=[b_gst])
                    E("dv", lambda e: e.tensor_scalar(
                        wg_b2[:, 0:1024], xin[0][:].bitcast(I32),
                        INV127, None, ALU.mult),
                      reads=[b_xin[0]], writes=[b_wg[0]])
                    E("dv", lambda e: e.tensor_scalar(
                        wg_b2[:, 1024:2048], gst[:, 1024:2048],
                        INV127, None, ALU.mult),
                      reads=[b_gst], writes=[b_wg[1]])
                    E("gp", dma=True, fn=lambda e: e.dma_start(
                        out=ust[:, 1024:2048],
                        in_=d_wu.ap()[128:256, :]),
                      writes=[b_ust])
                    E("dv", lambda e: e.tensor_scalar(
                        wu_b2[:, 0:1024], xin[1][:].bitcast(I32),
                        INV127, None, ALU.mult),
                      reads=[b_xin[1]], writes=[b_wu[0]])
                    E("dv", lambda e: e.tensor_scalar(
                        wu_b2[:, 1024:2048], ust[:, 1024:2048],
                        INV127, None, ALU.mult),
                      reads=[b_ust], writes=[b_wu[1]])
                else:
                    E("sp", dma=True, fn=lambda e, hb0=hb0: e.dma_start(
                        out=gst[:],
                        in_=bass.AP(d_wg, hb0 * 128 * 1024,
                                    [[1024, 128], [128 * 1024, 2], [1, 1024]])),
                      writes=[b_gst])
                    E("dv", lambda e: e.tensor_scalar(
                        wg_b2[:], gst[:], INV127, None, ALU.mult),
                      reads=[b_gst], writes=[b_wg[0], b_wg[1]])
                    E("gp", dma=True, fn=lambda e, hb0=hb0: e.dma_start(
                        out=ust[:],
                        in_=bass.AP(d_wu, hb0 * 128 * 1024,
                                    [[1024, 128], [128 * 1024, 2], [1, 1024]])),
                      writes=[b_ust])
                    E("dv", lambda e: e.tensor_scalar(
                        wu_b2[:], ust[:], INV127, None, ALU.mult),
                      reads=[b_ust], writes=[b_wu[0], b_wu[1]])
                for hb in (hb0, hb0 + 1):
                    hoff = (hb % 2) * 1024
                    for tc in range(2):
                        gtc = hf * 2 + tc
                        gbk, ubk = tc, 2 + tc
                        for jb in range(NJ):
                            E("pe", lambda e, hoff=hoff, jb=jb, gtc=gtc,
                              gbk=gbk: e.matmul(
                                  BK(gbk),
                                  wg_b2[:, hoff + jb * 128:
                                        hoff + (jb + 1) * 128],
                                  xnT[:, jb * S + gtc * 512:
                                      jb * S + gtc * 512 + 512],
                                  start=(jb == 0), stop=(jb == NJ - 1)),
                              reads=[b_wg[hb % 2], b_xnT[(jb, gtc)]],
                              acc_first=[b_pb[gbk]] if jb == 0 else (),
                              acc_last=[b_pb[gbk]] if jb == NJ - 1 else ())
                        for jb in range(NJ):
                            E("pe", lambda e, hoff=hoff, jb=jb, gtc=gtc,
                              ubk=ubk: e.matmul(
                                  BK(ubk),
                                  wu_b2[:, hoff + jb * 128:
                                        hoff + (jb + 1) * 128],
                                  xnT[:, jb * S + gtc * 512:
                                      jb * S + gtc * 512 + 512],
                                  start=(jb == 0), stop=(jb == NJ - 1)),
                              reads=[b_wu[hb % 2], b_xnT[(jb, gtc)]],
                              acc_first=[b_pb[ubk]] if jb == 0 else (),
                              acc_last=[b_pb[ubk]] if jb == NJ - 1 else ())
                        E("ac", lambda e, tc=tc, gbk=gbk: e.activation(
                            out=gs[tc][:], in_=BK(gbk), func=AF.Silu),
                          reads=[b_pb[gbk]], writes=[b_gs[tc], b_xin[tc]])
                        E("dv", lambda e, hb=hb, tc=tc, ubk=ubk: e.tensor_tensor(
                            Ht[:, hb * 1024 + tc * 512:
                               hb * 1024 + tc * 512 + 512],
                            gs[tc][:], BK(ubk), ALU.mult),
                          reads=[b_gs[tc], b_xin[tc], b_pb[ubk]],
                          writes=[b_H[(hb, tc)]])
                if g2 % 2 == 0:
                    g4 = g2 // 2
                    E("sp" if g4 % 2 == 0 else "gp", dma=True,
                      fn=lambda e, g4=g4: e.dma_start(
                          out=dst[:],
                          in_=bass.AP(d_wd, g4 * 4 * 128 * 1024,
                                      [[1024, 128], [128 * 1024, 4], [1, 512]])),
                      writes=[b_dst])
                    E("dv", lambda e, g4=g4: e.tensor_scalar(
                        wd_b[:, g4 * 2048:(g4 + 1) * 2048], dst[:],
                        INV127, None, ALU.mult),
                      reads=[b_dst],
                      writes=[b_wd[j] for j in range(4 * g4, 4 * g4 + 4)])

            def wo_ap(g4, dc):
                return bass.AP(d_wo, g4 * 4 * 128 * 1024 + dc * 512,
                               [[1024, 128], [128 * 1024, 4], [1, 512]])

            def wd_ap(g4, dc):
                return bass.AP(d_wd, g4 * 4 * 128 * 1024 + dc * 512,
                               [[1024, 128], [128 * 1024, 4], [1, 512]])

            def wo_cp(dst_half, src_ap, src_buf):
                E("dv", lambda e, dst_half=dst_half, src_ap=src_ap:
                  e.tensor_copy(
                      wo_b[:, dst_half * 2048:(dst_half + 1) * 2048], src_ap),
                  reads=[src_buf],
                  writes=[b_wo[j] for j in range(4 * dst_half,
                                                 4 * dst_half + 4)])

            def wd_dq(g4, stg, stg_buf):
                E("dv", lambda e, g4=g4, stg=stg: e.tensor_scalar(
                    wd_b[:, g4 * 2048:(g4 + 1) * 2048], stg[:],
                    INV127, None, ALU.mult),
                  reads=[stg_buf],
                  writes=[b_wd[j] for j in range(4 * g4, 4 * g4 + 4)])

            wd_rot = [(dst, b_dst), (ust, b_ust), (gst, b_gst)]

            for dc in range(2):
                if dc == 0:
                    E("sp", dma=True, fn=lambda e: e.dma_start(
                        out=ost_g[:], in_=wo_ap(0, 0)), writes=[b_gst])
                    E("gp", dma=True, fn=lambda e: e.dma_start(
                        out=ost_d[:], in_=wo_ap(1, 0)), writes=[b_dst])
                    wo_cp(0, ost_g[:], b_gst)
                    wo_cp(1, ost_d[:], b_dst)
                    E("sp", dma=True, fn=lambda e: e.dma_start(
                        out=ost_d[:], in_=wo_ap(0, 1)), writes=[b_dst])
                    E("gp", dma=True, fn=lambda e: e.dma_start(
                        out=ost_u[:], in_=wo_ap(1, 1)), writes=[b_ust])
                    E("sp", dma=True, fn=lambda e: e.dma_start(
                        out=gst[:], in_=wd_ap(0, 1)), writes=[b_gst])
                else:
                    wo_cp(0, ost_d[:], b_dst)
                    E("sp", dma=True, fn=lambda e: e.dma_start(
                        out=dst[:], in_=wd_ap(1, 1)), writes=[b_dst])
                    wo_cp(1, ost_u[:], b_ust)
                    E("gp", dma=True, fn=lambda e: e.dma_start(
                        out=ust[:], in_=wd_ap(2, 1)), writes=[b_ust])
                    wd_dq(0, gst, b_gst)
                    E("sp", dma=True, fn=lambda e: e.dma_start(
                        out=gst[:], in_=wd_ap(3, 1)), writes=[b_gst])
                    wd_dq(1, dst, b_dst)
                    E("gp", dma=True, fn=lambda e: e.dma_start(
                        out=dst[:], in_=wd_ap(4, 1)), writes=[b_dst])
                    wd_dq(2, ust, b_ust)
                    E("sp", dma=True, fn=lambda e: e.dma_start(
                        out=ust[:], in_=wd_ap(5, 1)), writes=[b_ust])
                    wd_dq(3, gst, b_gst)
                    wd_dq(4, dst, b_dst)
                    wd_dq(5, ust, b_ust)
                for ti in range(8):
                    if hf == 1 and weave_next is not None:
                        weave_next(dc * 8 + ti)
                    tb = hf * 8 + ti
                    xi = ti % 2
                    oi = (dc * 8 + ti) % 3
                    obk = 4 + (dc * 8 + ti) % 2
                    E("ac", dma=True, fn=lambda e, tb=tb, dc=dc, xi=xi: e.dma_start(
                        out=xr[xi][:],
                        in_=d_x.ap()[n * S + tb * 128: n * S + tb * 128 + 128,
                                     dc * 512:(dc + 1) * 512]),
                      writes=[b_xr[xi]])
                    E("pe", lambda e, dc=dc, obk=obk: e.matmul(
                        BK(obk), ones_row[:],
                        bo_b[0:1, dc * 512:(dc + 1) * 512],
                        start=True, stop=False),
                      reads=[b_ones, b_bo_b],
                      acc_first=[b_pb[obk]])

                    def wo_mms(obk=obk, tb=tb, last=False):
                        for jb in range(NJ):
                            E("pe", lambda e, jb=jb, tb=tb, obk=obk,
                              last=last: e.matmul(
                                  BK(obk),
                                  QT[:, jb * S + tb * 128:
                                     jb * S + tb * 128 + 128],
                                  wo_b[:, jb * 512:(jb + 1) * 512],
                                  start=False,
                                  stop=(last and jb == NJ - 1)),
                              reads=[b_QT[(jb, tb // 4)], b_wo[jb]],
                              acc_last=[b_pb[obk]]
                              if (last and jb == NJ - 1) else ())

                    def wd_mms(obk=obk, ti=ti, last=False):
                        for hb in range(NHB):
                            E("pe", lambda e, hb=hb, ti=ti, obk=obk,
                              last=last: e.matmul(
                                  BK(obk),
                                  Ht[:, hb * 1024 + ti * 128:
                                     hb * 1024 + ti * 128 + 128],
                                  wd_b[:, hb * 512:(hb + 1) * 512],
                                  start=False,
                                  stop=(last and hb == NHB - 1)),
                              reads=[b_H[(hb, ti // 4)], b_wd[hb]],
                              acc_last=[b_pb[obk]]
                              if (last and hb == NHB - 1) else ())

                    if dc == 0:
                        wd_mms(last=False)
                        wo_mms(last=True)
                    else:
                        wo_mms(last=False)
                        wd_mms(last=True)
                    E("dv", lambda e, xi=xi, oi=oi, obk=obk: e.tensor_add(
                        out4[oi][:], xr[xi][:], BK(obk)),
                      reads=[b_xr[xi], b_pb[obk]], writes=[b_out4[oi]])
                    E("gp", dma=True, fn=lambda e, tb=tb, dc=dc, oi=oi: e.dma_start(
                        out=d_y.ap()[n * S + tb * 128: n * S + tb * 128 + 128,
                                     dc * 512:(dc + 1) * 512],
                        in_=out4[oi][:]),
                      reads=[b_out4[oi]])

        if weave_next is not None:
            weave_next(NT)
            weave_next(NT + 1)
        ffn.close()

    ln1_slot_fn, _ = make_ln(1)

    emit_elem(0, weave_next=ln1_slot_fn)
    emit_elem(1, own_ln=False)
    mk.final_wait("gp")

    with nc.Block() as block:
        def section(eng_name):
            def body(EO):
                wm = {}
                for waits, fn, inc in mk.th[eng_name]:
                    for s in sorted(waits):
                        v = waits[s]
                        if wm.get(s, 0) < v:
                            EO.wait_ge(sems[s], v)
                            wm[s] = v
                    if fn is not None:
                        fn(EO).then_inc(sems[inc[0]], inc[1])
            return body

        block.tensor(section("pe"))
        block.vector(section("dv"))
        block.scalar(section("ac"))
        block.sync(section("sp"))
        block.gpsimd(section("gp"))

    es.close()
    return nc


_NC_CACHE = None


def _get_nc():
    global _NC_CACHE
    if _NC_CACHE is None:
        _NC_CACHE = build_program()
    return _NC_CACHE


def kernel(L0_attn_x, L0_attn_Wq, L0_attn_Wk, L0_attn_Wv, L0_attn_Wo,
           L0_attn_bo, L0_ffn_w_gate, L0_ffn_w_up, L0_ffn_w_down):
    x = np.ascontiguousarray(np.asarray(L0_attn_x, dtype=np.float32))
    wq = np.asarray(L0_attn_Wq, dtype=np.float32)
    wk = np.asarray(L0_attn_Wk, dtype=np.float32)
    wv = np.asarray(L0_attn_Wv, dtype=np.float32)
    wo = np.asarray(L0_attn_Wo, dtype=np.float32)
    bo = np.ascontiguousarray(
        np.asarray(L0_attn_bo, dtype=np.float32)).reshape(1, D)
    wg = np.asarray(L0_ffn_w_gate, dtype=np.int32)
    wu = np.asarray(L0_ffn_w_up, dtype=np.int32)
    wd = np.asarray(L0_ffn_w_down, dtype=np.int32)

    def qk_layout(w):
        return np.ascontiguousarray(
            w.reshape(NJ, 128, NJ, 128).transpose(0, 3, 2, 1).reshape(D, D))

    def gu_layout(w):
        return np.ascontiguousarray(
            w.reshape(NHB, 128, NJ, 128).transpose(0, 3, 2, 1).reshape(FH, D))

    wq_h = qk_layout(wq)
    wk_h = qk_layout(wk)
    wv_h = np.ascontiguousarray(wv.T)
    wo_h = np.ascontiguousarray(wo.T)
    wg_h = gu_layout(wg)
    wu_h = gu_layout(wu)
    wd_h = np.ascontiguousarray(wd.T)
    ident = np.eye(128, dtype=np.float32)

    nc = _get_nc()
    in_maps = []
    for c in range(8):
        in_maps.append({
            "x": np.ascontiguousarray(x[2 * c: 2 * c + 2].reshape(NB * S, D)),
            "wq": wq_h, "wk": wk_h, "wv": wv_h, "wo": wo_h, "bo": bo,
            "wg": wg_h, "wu": wu_h, "wd": wd_h, "ident": ident,
        })
    res = run_bass_kernel_spmd(nc, in_maps, core_ids=list(range(8)))
    out = np.empty((16, S, D), dtype=np.float32)
    for c in range(8):
        out[2 * c: 2 * c + 2] = np.asarray(
            res.results[c]["y"]).reshape(NB, S, D)
    return out


# version 22
# speedup vs baseline: 1.3372x; 1.0529x over previous
import numpy as np
from contextlib import ExitStack
import concourse.bass as bass
import concourse.mybir as mybir
from concourse.bass_utils import run_bass_kernel_spmd

F32 = mybir.dt.float32
BF16 = mybir.dt.bfloat16
I32 = mybir.dt.int32
AF = mybir.ActivationFunctionType
AXX = mybir.AxisListType.X
ALU = mybir.AluOpType

D = 1024
S = 2048
NB = 2
NH = 16
HD = 64
FH = 3072
NT = S // 128
NJ = D // 128
NHB = FH // 128
EPS = 1e-5
INV127 = float(np.float32(1.0) / np.float32(127.0))


class Buf:
    __slots__ = ("w", "r")

    def __init__(self):
        self.w = None
        self.r = {}


class MK:
    ENG = ("pe", "dv", "ac", "sp", "gp")
    NLANES = {"sp": 8, "gp": 4, "ac": 2}

    def __init__(self):
        self.th = {e: [] for e in self.ENG}
        self.cnt = {e: 0 for e in self.ENG}
        for q, nl in self.NLANES.items():
            for l in range(nl):
                self.cnt[f"{q}{l}"] = 0
        self.dmacnt = {"sp": 0, "gp": 0, "ac": 0}
        self.pending = {}

    def all_sems(self):
        return list(self.cnt)

    def emit(self, eng, fn=None, reads=(), writes=(), acc_first=(), acc_last=(),
             extra=(), dma=False):
        waits = {}

        def need(tok):
            if tok is not None:
                s, v = tok
                if waits.get(s, 0) < v:
                    waits[s] = v

        for b in reads:
            need(b.w)
        for b in tuple(writes) + tuple(acc_first):
            need(b.w)
            for s, v in b.r.items():
                need((s, v))
        for t in extra:
            need(t)
        p = self.pending.pop(eng, None)
        if p:
            for s, v in p.items():
                need((s, v))
        if dma:
            i = self.dmacnt[eng]
            self.dmacnt[eng] = i + 1
            lane = i % self.NLANES[eng]
            semn = f"{eng}{lane}"
            if self.cnt[semn] > 0:
                need((semn, self.cnt[semn]))
            self.cnt[semn] += 16
            tok = (semn, self.cnt[semn])
            inc = (semn, 16)
        else:
            self.cnt[eng] += 1
            tok = (eng, self.cnt[eng])
            inc = (eng, 1)
        for b in reads:
            if b.r.get(tok[0], 0) < tok[1]:
                b.r[tok[0]] = tok[1]
        for b in tuple(writes) + tuple(acc_last):
            b.w = tok
            b.r = {}
        self.th[eng].append((waits, fn, inc))
        return tok

    def barrier(self):
        caps = {s: v for s, v in self.cnt.items() if v > 0}
        for e in self.ENG:
            p = self.pending.setdefault(e, {})
            for s, v in caps.items():
                if p.get(s, 0) < v:
                    p[s] = v

    def final_wait(self, eng):
        cap = dict(self.cnt)
        self.th[eng].append(
            ({s: v for s, v in cap.items() if v > 0}, None, None))


def build_program():
    nc = bass.Bass("TRN2", target_bir_lowering=False, monotonic_sem_count=0)
    d_x = nc.dram_tensor("x", [NB * S, D], F32, kind="ExternalInput")
    d_wq = nc.dram_tensor("wq", [D, D], F32, kind="ExternalInput")
    d_wk = nc.dram_tensor("wk", [D, D], F32, kind="ExternalInput")
    d_wv = nc.dram_tensor("wv", [D, D], F32, kind="ExternalInput")
    d_wo = nc.dram_tensor("wo", [D, D], F32, kind="ExternalInput")
    d_bo = nc.dram_tensor("bo", [1, D], F32, kind="ExternalInput")
    d_wg = nc.dram_tensor("wg", [FH, D], I32, kind="ExternalInput")
    d_wu = nc.dram_tensor("wu", [FH, D], I32, kind="ExternalInput")
    d_wd = nc.dram_tensor("wd", [FH, D], I32, kind="ExternalInput")
    d_id = nc.dram_tensor("ident", [128, 128], F32, kind="ExternalInput")
    d_y = nc.dram_tensor("y", [NB * S, D], F32, kind="ExternalOutput")

    es = ExitStack()
    SB = lambda nm, sh, dt: es.enter_context(nc.sbuf_tensor(nm, sh, dt))
    mk = MK()
    E = mk.emit

    xnT = SB("xnT", [128, NJ * S], BF16)
    QT = SB("QT", [128, NJ * S], BF16)
    xin = [SB(f"xin{i}", [128, D], F32) for i in range(3)]
    xr = [SB(f"xr{i}", [128, 512], F32) for i in range(2)]
    out4 = [SB(f"out{i}", [128, 512], F32) for i in range(3)]
    ident_b = SB("idb", [128, 128], BF16)
    ones_row = SB("ones", [1, 128], BF16)
    bo_b = SB("bob", [1, D], BF16)
    negc = SB("negc", [128, NH], F32)
    stat = [SB(f"stat{i}", [128, 16], F32) for i in range(2)]
    xnb = [SB(f"xnb{i}", [128, D], BF16) for i in range(2)]
    gs = None
    cm_f = [SB(f"cmf{i}", [128, 1], F32) for i in range(2)]
    cm_b = [SB(f"cmb{i}", [128, 1], BF16) for i in range(2)]
    c_f = [SB(f"cf{i}", [1, 1], F32) for i in range(2)]
    c_b = [SB(f"cb{i}", [1, 1], BF16) for i in range(2)]
    rc_b = [SB(f"rcb{i}", [1, 512], BF16) for i in range(2)]
    rc_f = [out4[0][0:1, :], out4[1][0:1, :]]
    b_rc_f = [None, None]
    pbig = [es.enter_context(nc.psum_tensor(f"pbig{i}", [128, 1024], F32))
            for i in range(2)]
    psm = [es.enter_context(nc.psum_tensor(f"psm{i}", [128, 512], F32))
           for i in range(4)]

    def BK(k, r0=0, r1=128, c0=0, c1=512):
        if k < 4:
            t = pbig[k // 2]
            off = (k % 2) * 512
            return t[r0:r1, off + c0: off + c1]
        return psm[k - 4][r0:r1, c0:c1]
    sems = {s: es.enter_context(nc.semaphore(f"sem_{s}")) for s in mk.all_sems()}

    B = Buf
    b_xin = [B(), B(), B()]
    b_stat = [[B() for _ in range(8)] for _ in range(2)]
    b_xnb = [B(), B()]
    b_xnT = {(j, tc): B() for j in range(NJ) for tc in range(4)}
    b_QT = {(j, qc): B() for j in range(NJ) for qc in range(4)}
    b_pb = [B() for _ in range(8)]
    b_ident_f, b_ident_b, b_ones = B(), B(), B()
    b_bo_f, b_bo_b = B(), B()
    b_negc = [B() for _ in range(NH)]
    b_gs = [B(), B()]
    b_cm_f = [B(), B()]
    b_cm_b = [B(), B()]
    b_c_f = [B(), B()]
    b_c_b = [B(), B()]
    b_rc_b = [B(), B()]
    b_av_sb = [B(), B()]
    b_xr = [B(), B()]
    b_out4 = [B() for _ in range(3)]
    b_rc_f[0] = b_out4[0]
    b_rc_f[1] = b_out4[1]

    E("sp", dma=True, fn=lambda e: e.dma_start(out=ident_f, in_=d_id.ap()),
      writes=[b_ident_f, b_xin[1]])
    bo_f = xin[0][0:1, :]
    ident_f = xin[1][:, 0:128]
    E("sp", dma=True, fn=lambda e: e.dma_start(out=bo_f, in_=d_bo.ap()),
      writes=[b_bo_f, b_xin[0]])
    E("dv", lambda e: e.tensor_copy(ident_b[:], ident_f),
      reads=[b_ident_f, b_xin[1]], writes=[b_ident_b])
    E("dv", lambda e: e.memset(ones_row[:], 1.0), writes=[b_ones])
    E("dv", lambda e: e.tensor_copy(bo_b[:], bo_f),
      reads=[b_bo_f, b_xin[0]], writes=[b_bo_b])

    ROW = NJ * S

    def make_ln(n):
            def ln_s1(tb):
                i = tb % 2
                xi = tb % 3
                st_ = b_stat[i]
                E("sp", dma=True, fn=lambda e, xi=xi, tb=tb: e.dma_start(
                    out=xin[xi][:],
                    in_=d_x.ap()[n * S + tb * 128: n * S + tb * 128 + 128, :]),
                  writes=[b_xin[xi]])
                E("dv", lambda e, i=i, xi=xi: e.bn_stats(
                    stat[i][:, 0:6], xin[xi][:, 0:512]),
                  reads=[b_xin[xi]], writes=[st_[0]])
                E("dv", lambda e, i=i, xi=xi: e.bn_stats(
                    stat[i][:, 6:12], xin[xi][:, 512:1024]),
                  reads=[b_xin[xi]], writes=[st_[1]])
                E("dv", lambda e, i=i: e.bn_aggr(
                    stat[i][:, 12:14], stat[i][:, 0:12]),
                  reads=[st_[0], st_[1]], writes=[st_[2]])
                E("dv", lambda e, i=i: e.tensor_scalar(
                    stat[i][:, 13:14], stat[i][:, 13:14], EPS, None, ALU.add),
                  reads=[st_[2]], writes=[st_[3]])

            def ln_s2(tb):
                i = tb % 2
                xi = tb % 3
                st_ = b_stat[i]
                E("ac", lambda e, i=i: e.activation(
                    out=stat[i][:, 14:15], in_=stat[i][:, 13:14], func=AF.Sqrt),
                  reads=[st_[3]], writes=[st_[4]])
                E("dv", lambda e, i=i: e.reciprocal(
                    stat[i][:, 15:16], stat[i][:, 14:15]),
                  reads=[st_[4]], writes=[st_[5]])
                E("dv", lambda e, i=i: e.scalar_tensor_tensor(
                    stat[i][:, 14:15], stat[i][:, 12:13], -1.0, stat[i][:, 15:16],
                    ALU.mult, ALU.mult),
                  reads=[st_[2], st_[5]], writes=[st_[6]])
                E("ac", lambda e, i=i, xi=xi: e.activation(
                    out=xnb[i][:], in_=xin[xi][:], func=AF.Identity,
                    bias=stat[i][:, 14:15], scale=stat[i][:, 15:16]),
                  reads=[b_xin[xi], st_[5], st_[6]], writes=[b_xnb[i]])

            def ln_s3(tb):
                i = tb % 2
                bA, bB = 6, 7
                for half, bk in ((0, bA), (1, bB)):
                    for q in range(4):
                        db = half * 4 + q
                        E("pe", lambda e, i=i, db=db, bk=bk, q=q: e.matmul(
                            BK(bk, c0=q * 128, c1=(q + 1) * 128),
                            xnb[i][:, db * 128:(db + 1) * 128], ident_b[:],
                            start=True, stop=True),
                          reads=[b_xnb[i], b_ident_b],
                          acc_first=[b_pb[bk]] if q == 0 else (),
                          acc_last=[b_pb[bk]] if q == 3 else ())
                    eng = "dv" if half == 0 else "ac"
                    if eng == "dv":
                        E("dv", lambda e, tb=tb, half=half, bk=bk: e.tensor_copy(
                            bass.AP(xnT, half * 4 * S + tb * 128,
                                    [[ROW, 128], [S, 4], [1, 128]]),
                            BK(bk)),
                          reads=[b_pb[bk]],
                          writes=[b_xnT[(half * 4 + d, tb // 4)] for d in range(4)])
                    else:
                        E("ac", lambda e, tb=tb, half=half, bk=bk: e.copy(
                            bass.AP(xnT, half * 4 * S + tb * 128,
                                    [[ROW, 128], [S, 4], [1, 128]]),
                            BK(bk)),
                          reads=[b_pb[bk]],
                          writes=[b_xnT[(half * 4 + d, tb // 4)] for d in range(4)])

            def ln_slot(tb, vhook=None):
                if tb < NT:
                    ln_s1(tb)
                if tb >= 1 and tb - 1 < NT:
                    ln_s2(tb - 1)
                if tb >= 2 and tb - 2 < NT:
                    ln_s3(tb - 2)
                    if vhook is not None:
                        vhook(tb - 2)

            def ln_pipeline(vhook=None):
                for tb in range(NT + 2):
                    ln_slot(tb, vhook)

            return ln_slot, ln_pipeline

    def emit_elem(n, weave_next=None, own_ln=True):
        ln_slot, ln_pipeline = make_ln(n)
        att = ExitStack()
        KT = att.enter_context(nc.sbuf_tensor(f"KT{n}", [128, NJ * S], BF16))
        Vaug = att.enter_context(
            nc.sbuf_tensor(f"Vaug{n}", [128, NT * NH * 65 + 64], BF16))
        wbuf = att.enter_context(
            nc.sbuf_tensor(f"wbuf{n}", [128, 8 * 1024], BF16))
        wstage = [att.enter_context(
            nc.sbuf_tensor(f"wst{n}_{i}", [128, 1024], F32)) for i in range(3)]
        expS = att.enter_context(
            nc.sbuf_tensor(f"expS{n}", [128, 4 * 1024], BF16))
        av_sb = [att.enter_context(
            nc.sbuf_tensor(f"avsb{n}_{i}", [65, 512], F32)) for i in range(2)]
        b_KT = {(j, tc): B() for j in range(NJ) for tc in range(4)}
        b_Vaug = [B() for _ in range(NT)]
        b_wslot = [B() for _ in range(8)]
        b_wstage = [B(), B(), B()]
        b_expS = [B() for _ in range(4)]
        VROW = NT * NH * 65
        VPITCH = VROW + 64

        def att_scope_open():
            E("dv", lambda e: e.memset(
                bass.AP(Vaug, 64, [[VPITCH, 128], [65, NT * NH]]), 1.0),
              writes=b_Vaug)
            E("dv", lambda e: e.memset(Vaug[:, VROW:VROW + 64], 0.0),
              writes=[b_Vaug[NT - 1]])

            def vload(jb):
                E("sp", dma=True, fn=lambda e, jb=jb: e.dma_start(
                    out=wstage[jb % 3][:],
                    in_=d_wv.ap()[jb * 128:(jb + 1) * 128, :]),
                  writes=[b_wstage[jb % 3]])
            vload(0)
            vload(1)
            for jb in range(NJ):
                E("dv", lambda e, jb=jb: e.tensor_copy(
                    wbuf[:, jb * 1024:(jb + 1) * 1024], wstage[jb % 3][:]),
                  reads=[b_wstage[jb % 3]], writes=[b_wslot[jb]])
                if jb + 2 < NJ:
                    vload(jb + 2)

        def v_pass_tb(tb):
            for dc in range(2):
                bk = (tb * 2 + dc) % 2
                for jb in range(NJ):
                    E("pe", lambda e, tb=tb, dc=dc, jb=jb, bk=bk: e.matmul(
                        BK(bk),
                        xnT[:, jb * S + tb * 128: jb * S + tb * 128 + 128],
                        wbuf[:, jb * 1024 + dc * 512:
                             jb * 1024 + dc * 512 + 512],
                        start=(jb == 0), stop=(jb == NJ - 1)),
                      reads=[b_wslot[jb], b_xnT[(jb, tb // 4)]],
                      acc_first=[b_pb[bk]] if jb == 0 else (),
                      acc_last=[b_pb[bk]] if jb == NJ - 1 else ())
                E("dv", lambda e, tb=tb, dc=dc, bk=bk: e.tensor_copy(
                    bass.AP(Vaug, tb * NH * 65 + dc * 8 * 65,
                            [[VPITCH, 128], [65, 8], [1, 64]]),
                    BK(bk)),
                  reads=[b_pb[bk]], writes=[b_Vaug[tb]])

        if n == 0:
            mk.barrier()
            att_scope_open()
            ln_pipeline(vhook=v_pass_tb)
        else:
            if own_ln:
                ln_pipeline(None)
            mk.barrier()
            att_scope_open()
            for tb in range(NT):
                v_pass_tb(tb)

        def proj(dram, dest, destbufs, kscale, hook=None):
            def wload(ib):
                E("sp", dma=True, fn=lambda e, ib=ib: e.dma_start(
                    out=wstage[ib % 3][:],
                    in_=dram.ap()[ib * 128:(ib + 1) * 128, :]),
                  writes=[b_wstage[ib % 3]])
            wload(0)
            wload(1)
            for ib in range(NJ):
                E("dv", lambda e, ib=ib: e.tensor_copy(
                    wbuf[:, ib * 1024:(ib + 1) * 1024], wstage[ib % 3][:]),
                  reads=[b_wstage[ib % 3]], writes=[b_wslot[ib]])
                if ib + 2 < NJ:
                    wload(ib + 2)
                if hook is not None and ib > 0:
                    hook(ib - 1)
                for jb in range(NJ):
                    for tc in range(4):
                        E("pe", lambda e, ib=ib, tc=tc, jb=jb: e.matmul(
                            BK(tc),
                            wbuf[:, ib * 1024 + jb * 128:
                                 ib * 1024 + jb * 128 + 128],
                            xnT[:, jb * S + tc * 512:
                                jb * S + tc * 512 + 512],
                            start=(jb == 0), stop=(jb == NJ - 1)),
                          reads=[b_wslot[ib], b_xnT[(jb, tc)]],
                          acc_first=[b_pb[tc]] if jb == 0 else (),
                          acc_last=[b_pb[tc]] if jb == NJ - 1 else ())
                for tc in range(4):
                    if kscale is None:
                        E("ac", lambda e, ib=ib, tc=tc: e.copy(
                            dest[:, ib * S + tc * 512:
                                 ib * S + tc * 512 + 512],
                            BK(tc)),
                          reads=[b_pb[tc]], writes=[destbufs[(ib, tc)]])
                    else:
                        E("dv", lambda e, ib=ib, tc=tc: e.tensor_scalar(
                            dest[:, ib * S + tc * 512:
                                 ib * S + tc * 512 + 512],
                            BK(tc), kscale, None, ALU.mult),
                          reads=[b_pb[tc]], writes=[destbufs[(ib, tc)]])

        def prepass(p):
            sl = p % 2
            for hh in range(2):
                pr = hh * 64
                E("pe", lambda e, p=p, pr=pr, hh=hh: e.matmul(
                    BK(4 + hh),
                    KT[pr:pr + 64, p * S: p * S + 128],
                    QT[pr:pr + 64, p * S: p * S + 512],
                    start=True, stop=True, tile_position=(pr, 0)),
                  reads=[b_KT[(p, 0)], b_QT[(p, 0)]],
                  acc_first=[b_pb[4 + hh]], acc_last=[b_pb[4 + hh]])
            E("dv", lambda e, sl=sl: e.reduce_max(
                cm_f[sl][:], BK(4), axis=AXX),
              reads=[b_pb[4]], writes=[b_cm_f[sl]])
            E("dv", lambda e, sl=sl: e.reduce_max(
                stat[sl][:, 0:1], BK(5), axis=AXX),
              reads=[b_pb[5]], writes=[b_stat[sl][0]])
            E("dv", lambda e, sl=sl: e.tensor_max(
                cm_f[sl][:], cm_f[sl][:], stat[sl][:, 0:1]),
              reads=[b_cm_f[sl], b_stat[sl][0]], writes=[b_cm_f[sl]])
            E("dv", lambda e, sl=sl: e.tensor_copy(
                cm_b[sl][:], cm_f[sl][:]),
              reads=[b_cm_f[sl]], writes=[b_cm_b[sl]])
            E("pe", lambda e, sl=sl: e.matmul(
                BK(6, r1=1, c1=128), cm_b[sl][:], ident_b[:],
                start=True, stop=True),
              reads=[b_cm_b[sl], b_ident_b],
              acc_first=[b_pb[6]], acc_last=[b_pb[6]])
            E("dv", lambda e, sl=sl: e.reduce_max(
                c_f[sl][:], BK(6, r1=1, c1=128), axis=AXX),
              reads=[b_pb[6]], writes=[b_c_f[sl]])
            E("dv", lambda e, sl=sl: e.tensor_scalar(
                c_b[sl][:], c_f[sl][:], -1.0, None, ALU.mult),
              reads=[b_c_f[sl]], writes=[b_c_b[sl]])
            E("pe", lambda e, sl=sl: e.matmul(
                BK(7, c1=1), ones_row[:], c_b[sl][:],
                start=True, stop=True),
              reads=[b_ones, b_c_b[sl]],
              acc_first=[b_pb[7]], acc_last=[b_pb[7]])
            E("ac", lambda e, p=p: e.copy(
                negc[:, p:p + 1], BK(7, c1=1)),
              reads=[b_pb[7]], writes=[b_negc[p]])

        proj(d_wq, QT, b_QT, None)
        proj(d_wk, KT, b_KT, 0.125, hook=prepass)
        prepass(NH // 2 - 1)

        pend = [None]

        def mk_tail_b(p, qc):
            def emit_tail_b():
                for hh in range(2):
                    pr = hh * 64
                    bck = 6 + hh
                    E("pe", lambda e, hh=hh, bck=bck: e.matmul(
                        BK(bck, r1=64), ones_row[0:1, 0:64], rc_b[hh][:],
                        start=True, stop=True),
                      reads=[b_ones, b_rc_b[hh]],
                      acc_first=[b_pb[bck]], acc_last=[b_pb[bck]])
                    E("dv", lambda e, hh=hh, pr=pr, bck=bck: e.tensor_tensor(
                        QT[pr:pr + 64,
                           p * S + qc * 512: p * S + qc * 512 + 512],
                        av_sb[hh][0:64, :], BK(bck, r1=64), ALU.mult),
                      reads=[b_av_sb[hh], b_pb[bck]], writes=[b_QT[(p, qc)]])
            return emit_tail_b

        def mk_chunk(p, qc):
            def sexp(kb):
                for hh in range(2):
                    pr = hh * 64
                    bk = (kb % 2) * 2 + hh
                    E("pe", lambda e, p=p, pr=pr, kb=kb, qc=qc, bk=bk:
                      e.matmul(
                          BK(bk),
                          KT[pr:pr + 64,
                             p * S + kb * 128: p * S + kb * 128 + 128],
                          QT[pr:pr + 64,
                             p * S + qc * 512: p * S + qc * 512 + 512],
                          start=True, stop=True, tile_position=(pr, 0)),
                      reads=[b_KT[(p, kb // 4)], b_QT[(p, qc)]],
                      acc_first=[b_pb[bk]], acc_last=[b_pb[bk]])
                par = kb % 2
                slot = kb % 4
                E("ac", lambda e, slot=slot, p=p, par=par: e.activation(
                    out=expS[:, slot * 1024:(slot + 1) * 1024],
                    in_=pbig[par][0:128, 0:1024], func=AF.Exp,
                    bias=negc[:, p:p + 1]),
                  reads=[b_pb[2 * par], b_pb[2 * par + 1], b_negc[p]],
                  writes=[b_expS[slot]])

            def avk(kb):
                for hh in range(2):
                    h = 2 * p + hh
                    slot = kb % 4
                    abk = 4 + hh
                    E("pe", lambda e, kb=kb, h=h, hh=hh, slot=slot,
                      abk=abk: e.matmul(
                          BK(abk),
                          Vaug[:, kb * NH * 65 + h * 65:
                               kb * NH * 65 + h * 65 + 128],
                          expS[:, slot * 1024 + hh * 512:
                               slot * 1024 + hh * 512 + 512],
                          start=(kb == 0), stop=(kb == NT - 1)),
                      reads=[b_Vaug[kb], b_Vaug[min(kb + 1, NT - 1)],
                             b_expS[slot]],
                      acc_first=[b_pb[abk]] if kb == 0 else (),
                      acc_last=[b_pb[abk]] if kb == NT - 1 else ())
            return sexp, avk

        chunks = [(p, qc) for p in range(NH // 2) for qc in range(4)]
        cfs = [mk_chunk(p, qc) for p, qc in chunks]
        for i, (p, qc) in enumerate(chunks):
            sexp, avk = cfs[i]
            for kb in range(1 if i > 0 else 0, NT):
                sexp(kb)
                if kb == 8 and pend[0] is not None:
                    pend[0]()
                    pend[0] = None
                if kb % 2 == 1 and kb >= 3:
                    avk(kb - 3)
                    avk(kb - 2)
                if kb == NT - 1 and i + 1 < len(chunks):
                    cfs[i + 1][0](0)
            avk(NT - 2)
            avk(NT - 1)
            for hh in range(2):
                E("dv", lambda e, hh=hh: e.tensor_copy(
                    av_sb[hh][:], BK(4 + hh, r1=65)),
                  reads=[b_pb[4 + hh]], writes=[b_av_sb[hh]])
            for hh in range(2):
                E("dv", lambda e, hh=hh: e.reciprocal(
                    rc_f[hh][:], av_sb[hh][64:65, :]),
                  reads=[b_av_sb[hh]], writes=[b_rc_f[hh]])
                E("dv", lambda e, hh=hh: e.tensor_copy(
                    rc_b[hh][:], rc_f[hh][:]),
                  reads=[b_rc_f[hh]], writes=[b_rc_b[hh]])
            pend[0] = mk_tail_b(p, qc)
        pend[0]()
        pend[0] = None

        att.close()

        ffn = ExitStack()
        Ht = ffn.enter_context(
            nc.sbuf_tensor(f"H{n}", [128, NHB * 1024], BF16))
        wg_b2 = ffn.enter_context(nc.sbuf_tensor(f"wgb{n}", [128, 2048], BF16))
        wu_b2 = ffn.enter_context(nc.sbuf_tensor(f"wub{n}", [128, 2048], BF16))
        gst = ffn.enter_context(nc.sbuf_tensor(f"gst{n}", [128, 2048], I32))
        ust = ffn.enter_context(nc.sbuf_tensor(f"ust{n}", [128, 2048], I32))
        wd_b = ffn.enter_context(
            nc.sbuf_tensor(f"wdb{n}", [128, NHB * 512], BF16))
        wo_b = ffn.enter_context(
            nc.sbuf_tensor(f"wob{n}", [128, 8 * 512], BF16))
        dst = ffn.enter_context(nc.sbuf_tensor(f"dst{n}_0", [128, 2048], I32))
        ost_g = gst[:].bitcast(F32)
        ost_d = dst[:].bitcast(F32)
        ost_u = ust[:].bitcast(F32)
        b_H = {(hb, tc): B() for hb in range(NHB) for tc in range(2)}
        b_wg = [B(), B()]
        b_wu = [B(), B()]
        b_gst = B()
        b_ust = B()
        b_wd = [B() for _ in range(NHB)]
        b_wo = [B() for _ in range(8)]
        b_dst = B()


        E("sp", dma=True, fn=lambda e: e.dma_start(
            out=xin[0][:].bitcast(I32), in_=d_wg.ap()[0:128, :]),
          writes=[b_xin[0]])
        E("gp", dma=True, fn=lambda e: e.dma_start(
            out=xin[1][:].bitcast(I32), in_=d_wu.ap()[0:128, :]),
          writes=[b_xin[1]])
        mk.barrier()
        gs = [xin[0][:, 0:512], xin[1][:, 0:512]]

        for hf in range(2):
            for g2 in range(NHB // 2):
                hb0 = 2 * g2
                if hf == 0 and g2 == 0:
                    E("sp", dma=True, fn=lambda e: e.dma_start(
                        out=gst[:, 1024:2048],
                        in_=d_wg.ap()[128:256, :]),
                      writes=[b_gst])
                    E("dv", lambda e: e.tensor_scalar(
                        wg_b2[:, 0:1024], xin[0][:].bitcast(I32),
                        INV127, None, ALU.mult),
                      reads=[b_xin[0]], writes=[b_wg[0]])
                    E("dv", lambda e: e.tensor_scalar(
                        wg_b2[:, 1024:2048], gst[:, 1024:2048],
                        INV127, None, ALU.mult),
                      reads=[b_gst], writes=[b_wg[1]])
                    E("gp", dma=True, fn=lambda e: e.dma_start(
                        out=ust[:, 1024:2048],
                        in_=d_wu.ap()[128:256, :]),
                      writes=[b_ust])
                    E("dv", lambda e: e.tensor_scalar(
                        wu_b2[:, 0:1024], xin[1][:].bitcast(I32),
                        INV127, None, ALU.mult),
                      reads=[b_xin[1]], writes=[b_wu[0]])
                    E("dv", lambda e: e.tensor_scalar(
                        wu_b2[:, 1024:2048], ust[:, 1024:2048],
                        INV127, None, ALU.mult),
                      reads=[b_ust], writes=[b_wu[1]])
                else:
                    E("sp", dma=True, fn=lambda e, hb0=hb0: e.dma_start(
                        out=gst[:],
                        in_=bass.AP(d_wg, hb0 * 128 * 1024,
                                    [[1024, 128], [128 * 1024, 2], [1, 1024]])),
                      writes=[b_gst])
                    E("dv", lambda e: e.tensor_scalar(
                        wg_b2[:], gst[:], INV127, None, ALU.mult),
                      reads=[b_gst], writes=[b_wg[0], b_wg[1]])
                    E("gp", dma=True, fn=lambda e, hb0=hb0: e.dma_start(
                        out=ust[:],
                        in_=bass.AP(d_wu, hb0 * 128 * 1024,
                                    [[1024, 128], [128 * 1024, 2], [1, 1024]])),
                      writes=[b_ust])
                    E("dv", lambda e: e.tensor_scalar(
                        wu_b2[:], ust[:], INV127, None, ALU.mult),
                      reads=[b_ust], writes=[b_wu[0], b_wu[1]])
                for hb in (hb0, hb0 + 1):
                    hoff = (hb % 2) * 1024
                    for tc in range(2):
                        gtc = hf * 2 + tc
                        gbk, ubk = tc, 2 + tc
                        for jb in range(NJ):
                            E("pe", lambda e, hoff=hoff, jb=jb, gtc=gtc,
                              gbk=gbk: e.matmul(
                                  BK(gbk),
                                  wg_b2[:, hoff + jb * 128:
                                        hoff + (jb + 1) * 128],
                                  xnT[:, jb * S + gtc * 512:
                                      jb * S + gtc * 512 + 512],
                                  start=(jb == 0), stop=(jb == NJ - 1)),
                              reads=[b_wg[hb % 2], b_xnT[(jb, gtc)]],
                              acc_first=[b_pb[gbk]] if jb == 0 else (),
                              acc_last=[b_pb[gbk]] if jb == NJ - 1 else ())
                        for jb in range(NJ):
                            E("pe", lambda e, hoff=hoff, jb=jb, gtc=gtc,
                              ubk=ubk: e.matmul(
                                  BK(ubk),
                                  wu_b2[:, hoff + jb * 128:
                                        hoff + (jb + 1) * 128],
                                  xnT[:, jb * S + gtc * 512:
                                      jb * S + gtc * 512 + 512],
                                  start=(jb == 0), stop=(jb == NJ - 1)),
                              reads=[b_wu[hb % 2], b_xnT[(jb, gtc)]],
                              acc_first=[b_pb[ubk]] if jb == 0 else (),
                              acc_last=[b_pb[ubk]] if jb == NJ - 1 else ())
                        E("ac", lambda e, tc=tc, gbk=gbk: e.activation(
                            out=gs[tc][:], in_=BK(gbk), func=AF.Silu),
                          reads=[b_pb[gbk]], writes=[b_gs[tc], b_xin[tc]])
                        E("dv", lambda e, hb=hb, tc=tc, ubk=ubk: e.tensor_tensor(
                            Ht[:, hb * 1024 + tc * 512:
                               hb * 1024 + tc * 512 + 512],
                            gs[tc][:], BK(ubk), ALU.mult),
                          reads=[b_gs[tc], b_xin[tc], b_pb[ubk]],
                          writes=[b_H[(hb, tc)]])
                if g2 % 2 == 0:
                    g4 = g2 // 2
                    E("sp" if g4 % 2 == 0 else "gp", dma=True,
                      fn=lambda e, g4=g4: e.dma_start(
                          out=dst[:],
                          in_=bass.AP(d_wd, g4 * 4 * 128 * 1024,
                                      [[1024, 128], [128 * 1024, 4], [1, 512]])),
                      writes=[b_dst])
                    E("dv", lambda e, g4=g4: e.tensor_scalar(
                        wd_b[:, g4 * 2048:(g4 + 1) * 2048], dst[:],
                        INV127, None, ALU.mult),
                      reads=[b_dst],
                      writes=[b_wd[j] for j in range(4 * g4, 4 * g4 + 4)])

            def wo_ap(g4, dc):
                return bass.AP(d_wo, g4 * 4 * 128 * 1024 + dc * 512,
                               [[1024, 128], [128 * 1024, 4], [1, 512]])

            def wd_ap(g4, dc):
                return bass.AP(d_wd, g4 * 4 * 128 * 1024 + dc * 512,
                               [[1024, 128], [128 * 1024, 4], [1, 512]])

            def wo_cp(dst_half, src_ap, src_buf):
                E("dv", lambda e, dst_half=dst_half, src_ap=src_ap:
                  e.tensor_copy(
                      wo_b[:, dst_half * 2048:(dst_half + 1) * 2048], src_ap),
                  reads=[src_buf],
                  writes=[b_wo[j] for j in range(4 * dst_half,
                                                 4 * dst_half + 4)])

            def wd_dq(g4, stg, stg_buf):
                E("dv", lambda e, g4=g4, stg=stg: e.tensor_scalar(
                    wd_b[:, g4 * 2048:(g4 + 1) * 2048], stg[:],
                    INV127, None, ALU.mult),
                  reads=[stg_buf],
                  writes=[b_wd[j] for j in range(4 * g4, 4 * g4 + 4)])

            wd_rot = [(dst, b_dst), (ust, b_ust), (gst, b_gst)]

            for dc in range(2):
                if dc == 0:
                    E("sp", dma=True, fn=lambda e: e.dma_start(
                        out=ost_g[:], in_=wo_ap(0, 0)), writes=[b_gst])
                    E("gp", dma=True, fn=lambda e: e.dma_start(
                        out=ost_d[:], in_=wo_ap(1, 0)), writes=[b_dst])
                    wo_cp(0, ost_g[:], b_gst)
                    wo_cp(1, ost_d[:], b_dst)
                    E("sp", dma=True, fn=lambda e: e.dma_start(
                        out=ost_d[:], in_=wo_ap(0, 1)), writes=[b_dst])
                    E("gp", dma=True, fn=lambda e: e.dma_start(
                        out=ost_u[:], in_=wo_ap(1, 1)), writes=[b_ust])
                    E("sp", dma=True, fn=lambda e: e.dma_start(
                        out=gst[:], in_=wd_ap(0, 1)), writes=[b_gst])
                else:
                    wo_cp(0, ost_d[:], b_dst)
                    E("sp", dma=True, fn=lambda e: e.dma_start(
                        out=dst[:], in_=wd_ap(1, 1)), writes=[b_dst])
                    wo_cp(1, ost_u[:], b_ust)
                    E("gp", dma=True, fn=lambda e: e.dma_start(
                        out=ust[:], in_=wd_ap(2, 1)), writes=[b_ust])
                    wd_dq(0, gst, b_gst)
                    E("sp", dma=True, fn=lambda e: e.dma_start(
                        out=gst[:], in_=wd_ap(3, 1)), writes=[b_gst])
                    wd_dq(1, dst, b_dst)
                    E("gp", dma=True, fn=lambda e: e.dma_start(
                        out=dst[:], in_=wd_ap(4, 1)), writes=[b_dst])
                    wd_dq(2, ust, b_ust)
                    E("sp", dma=True, fn=lambda e: e.dma_start(
                        out=ust[:], in_=wd_ap(5, 1)), writes=[b_ust])
                    wd_dq(3, gst, b_gst)
                    wd_dq(4, dst, b_dst)
                    wd_dq(5, ust, b_ust)
                for ti in range(8):
                    if hf == 1 and weave_next is not None:
                        weave_next(dc * 8 + ti)
                    tb = hf * 8 + ti
                    xi = ti % 2
                    oi = (dc * 8 + ti) % 3
                    obk = 4 + (dc * 8 + ti) % 2
                    E("ac", dma=True, fn=lambda e, tb=tb, dc=dc, xi=xi: e.dma_start(
                        out=xr[xi][:],
                        in_=d_x.ap()[n * S + tb * 128: n * S + tb * 128 + 128,
                                     dc * 512:(dc + 1) * 512]),
                      writes=[b_xr[xi]])
                    E("pe", lambda e, dc=dc, obk=obk: e.matmul(
                        BK(obk), ones_row[:],
                        bo_b[0:1, dc * 512:(dc + 1) * 512],
                        start=True, stop=False),
                      reads=[b_ones, b_bo_b],
                      acc_first=[b_pb[obk]])

                    def wo_mms(obk=obk, tb=tb, last=False):
                        for jb in range(NJ):
                            E("pe", lambda e, jb=jb, tb=tb, obk=obk,
                              last=last: e.matmul(
                                  BK(obk),
                                  QT[:, jb * S + tb * 128:
                                     jb * S + tb * 128 + 128],
                                  wo_b[:, jb * 512:(jb + 1) * 512],
                                  start=False,
                                  stop=(last and jb == NJ - 1)),
                              reads=[b_QT[(jb, tb // 4)], b_wo[jb]],
                              acc_last=[b_pb[obk]]
                              if (last and jb == NJ - 1) else ())

                    def wd_mms(obk=obk, ti=ti, last=False):
                        for hb in range(NHB):
                            E("pe", lambda e, hb=hb, ti=ti, obk=obk,
                              last=last: e.matmul(
                                  BK(obk),
                                  Ht[:, hb * 1024 + ti * 128:
                                     hb * 1024 + ti * 128 + 128],
                                  wd_b[:, hb * 512:(hb + 1) * 512],
                                  start=False,
                                  stop=(last and hb == NHB - 1)),
                              reads=[b_H[(hb, ti // 4)], b_wd[hb]],
                              acc_last=[b_pb[obk]]
                              if (last and hb == NHB - 1) else ())

                    if dc == 0:
                        wd_mms(last=False)
                        wo_mms(last=True)
                    else:
                        wo_mms(last=False)
                        wd_mms(last=True)
                    E("dv", lambda e, xi=xi, oi=oi, obk=obk: e.tensor_add(
                        out4[oi][:], xr[xi][:], BK(obk)),
                      reads=[b_xr[xi], b_pb[obk]], writes=[b_out4[oi]])
                    E("gp", dma=True, fn=lambda e, tb=tb, dc=dc, oi=oi: e.dma_start(
                        out=d_y.ap()[n * S + tb * 128: n * S + tb * 128 + 128,
                                     dc * 512:(dc + 1) * 512],
                        in_=out4[oi][:]),
                      reads=[b_out4[oi]])

        if weave_next is not None:
            weave_next(NT)
            weave_next(NT + 1)
        ffn.close()

    ln1_slot_fn, _ = make_ln(1)

    emit_elem(0, weave_next=ln1_slot_fn)
    emit_elem(1, own_ln=False)
    mk.final_wait("gp")

    with nc.Block() as block:
        def section(eng_name):
            def body(EO):
                wm = {}
                for waits, fn, inc in mk.th[eng_name]:
                    for s in sorted(waits):
                        v = waits[s]
                        if wm.get(s, 0) < v:
                            EO.wait_ge(sems[s], v)
                            wm[s] = v
                    if fn is not None:
                        fn(EO).then_inc(sems[inc[0]], inc[1])
            return body

        block.tensor(section("pe"))
        block.vector(section("dv"))
        block.scalar(section("ac"))
        block.sync(section("sp"))
        block.gpsimd(section("gp"))

    es.close()
    return nc


_NC_CACHE = None


def _get_nc():
    global _NC_CACHE
    if _NC_CACHE is None:
        _NC_CACHE = build_program()
    return _NC_CACHE


def kernel(L0_attn_x, L0_attn_Wq, L0_attn_Wk, L0_attn_Wv, L0_attn_Wo,
           L0_attn_bo, L0_ffn_w_gate, L0_ffn_w_up, L0_ffn_w_down):
    x = np.ascontiguousarray(np.asarray(L0_attn_x, dtype=np.float32))
    wq = np.asarray(L0_attn_Wq, dtype=np.float32)
    wk = np.asarray(L0_attn_Wk, dtype=np.float32)
    wv = np.asarray(L0_attn_Wv, dtype=np.float32)
    wo = np.asarray(L0_attn_Wo, dtype=np.float32)
    bo = np.ascontiguousarray(
        np.asarray(L0_attn_bo, dtype=np.float32)).reshape(1, D)
    wg = np.asarray(L0_ffn_w_gate, dtype=np.int32)
    wu = np.asarray(L0_ffn_w_up, dtype=np.int32)
    wd = np.asarray(L0_ffn_w_down, dtype=np.int32)

    def qk_layout(w):
        return np.ascontiguousarray(
            w.reshape(NJ, 128, NJ, 128).transpose(0, 3, 2, 1).reshape(D, D))

    def gu_layout(w):
        return np.ascontiguousarray(
            w.reshape(NHB, 128, NJ, 128).transpose(0, 3, 2, 1).reshape(FH, D))

    wq_h = qk_layout(wq)
    wk_h = qk_layout(wk)
    wv_h = np.ascontiguousarray(wv.T)
    wo_h = np.ascontiguousarray(wo.T)
    wg_h = gu_layout(wg)
    wu_h = gu_layout(wu)
    wd_h = np.ascontiguousarray(wd.T)
    ident = np.eye(128, dtype=np.float32)

    nc = _get_nc()
    in_maps = []
    for c in range(8):
        in_maps.append({
            "x": np.ascontiguousarray(x[2 * c: 2 * c + 2].reshape(NB * S, D)),
            "wq": wq_h, "wk": wk_h, "wv": wv_h, "wo": wo_h, "bo": bo,
            "wg": wg_h, "wu": wu_h, "wd": wd_h, "ident": ident,
        })
    res = run_bass_kernel_spmd(nc, in_maps, core_ids=list(range(8)))
    out = np.empty((16, S, D), dtype=np.float32)
    for c in range(8):
        out[2 * c: 2 * c + 2] = np.asarray(
            res.results[c]["y"]).reshape(NB, S, D)
    return out
